# version 03
# baseline (speedup 1.0000x reference)
import contextlib

import numpy as np
import ml_dtypes

import concourse.bass as bass
import concourse.mybir as mybir
from concourse.bass_utils import run_bass_kernel_spmd

F32 = mybir.dt.float32
BF16 = mybir.dt.bfloat16
I32 = mybir.dt.int32
AF = mybir.ActivationFunctionType
ALU = mybir.AluOpType
AX = mybir.AxisListType

T = 4096
D = 1024
DI = 2048
NST = 16
RK = 64
FF = 4096
ECD = D // 128
DC = DI // 128
FC = FF // 128
L1 = 512
NT1 = T // L1
L2 = 256
NT2 = T // L2

EPSQ = 1e-8
EPSLN = 1e-5

PHASES = ["P0", "M1", "M2", "M3", "AR1", "WQ1", "F1", "AR2", "WQ2", "F2"]
ENGINES = ["sync", "gpsimd", "vector", "scalar", "tensor"]


class Ctr:
    def __init__(self, handle):
        self.h = handle
        self.v = 0


class Prog:

    def __init__(self, nc, block, sems):
        self.nc = nc
        self.block = block
        self.sems = sems
        self.streams = {n: [] for n in ENGINES}

    def op(self, eng, fn, inc=None, n=1):
        if inc is not None:
            ctr = self.sems[inc]
            ctr.v += n
            nn = n

            def thunk(E, fn=fn, ctr=ctr, nn=nn):
                fn(E).then_inc(ctr.h, nn)
        else:
            def thunk(E, fn=fn):
                fn(E)
        self.streams[eng].append(thunk)
        return self.sems[inc].v if inc is not None else None

    def wait(self, eng, sem, val):
        if val <= 0:
            return
        h = self.sems[sem].h
        self.streams[eng].append(lambda E, h=h, val=val: E.wait_ge(h, val))

    def wait_now(self, eng, sem):
        self.wait(eng, sem, self.sems[sem].v)

    def snap(self):
        return {k: c.v for k, c in self.sems.items()}

    def wait_snap(self, eng, snapshot):
        for k, v in snapshot.items():
            self.wait(eng, k, v)

    def barrier_all(self):
        s = self.snap()
        for eng in ENGINES:
            self.wait_snap(eng, s)

    def flush(self):
        for name in ENGINES:
            st = self.streams[name]
            self.streams[name] = []
            if st:
                getattr(self.block, name)(lambda E, st=st: [f(E) for f in st])


def build_program(phase_limit="F2", debug_spills=False):
    nc = bass.Bass("TRN2", target_bir_lowering=False)
    nlim = PHASES.index(phase_limit)

    def active(ph):
        return PHASES.index(ph) <= nlim

    def din(name, shape, dtype=F32):
        return nc.dram_tensor(name, shape, dtype, kind="ExternalInput").ap()

    xT = din("xT", [D, T])
    W_inT = din("W_inT", [D, 2 * DI])
    W_xTp = din("W_xTp", [DI, 128])
    W_dtT_i = din("W_dtT", [RK, DI])
    W_outT_i = din("W_outT", [DI, D])
    w1n_i = din("w1n", [FF, D])
    w2n_i = din("w2n", [D, FF])
    conv_w_i = din("conv_w", [DI, 4])
    conv_b_i = din("conv_b", [DI, 1])
    b_dt_i = din("b_dt", [DI, 1])
    A_log_i = din("A_log", [DI, NST])
    D_i = din("D_vec", [DI, 1])
    b1_i = din("b1", [FF, 1])
    b2_i = din("b2", [D, 1])
    eye_i = din("eye", [128, 128], BF16)
    sel_i = din("sel", [16, 16 * 128], BF16)

    outT = nc.dram_tensor("outT", [D, T], F32, kind="ExternalOutput").ap()

    spk = "ExternalOutput" if debug_spills else "Internal"
    xconv_sp = nc.dram_tensor("xconv_sp", [128, DC, T], BF16, kind="Internal").ap()
    sz_sp = nc.dram_tensor("sz_sp", [128, DC, T], BF16, kind="Internal").ap()
    y2_sp = nc.dram_tensor("y2_sp", [128, DC, T], BF16, kind="Internal").ap()
    x1_sp = nc.dram_tensor("x1_sp", [128, ECD, T], F32, kind=spk).ap()
    h_sp = nc.dram_tensor("h_sp", [128, FC, T], F32, kind=spk).ap()
    dbg_sp = nc.dram_tensor("dbg_sp", [128, DC, T], F32, kind=spk).ap()
    if debug_spills:
        xconv_dbg = nc.dram_tensor("xconv_dbg", [128, DC, T], F32, kind=spk).ap()
        sz_dbg = nc.dram_tensor("sz_dbg", [128, DC, T], F32, kind=spk).ap()
        y2_dbg = nc.dram_tensor("y2_dbg", [128, DC, T], F32, kind=spk).ap()
        w1i_dbg = nc.dram_tensor("w1i_dbg", [128, ECD, FF], F32, kind=spk).ap()
        xi_dbg = nc.dram_tensor("xi_dbg", [128, ECD, T], F32, kind=spk).ap()
        q1_dbg = nc.dram_tensor("q1_dbg", [128, 4 * FC], F32, kind=spk).ap()

    cc1_in = nc.dram_tensor("cc1_in", [128, 16], F32, kind="Internal").ap()
    cc1_out = nc.dram_tensor("cc1_out", [128, 16], F32, kind="Internal").ap()
    cc2_in = nc.dram_tensor("cc2_in", [128, 16], F32, kind="Internal").ap()
    cc2_out = nc.dram_tensor("cc2_out", [128, 16], F32, kind="Internal").ap()

    with contextlib.ExitStack() as top:
        e = top.enter_context

        xdbl_sb = e(nc.sbuf_tensor("xdbl_sb", [128, T], BF16))
        A_sb = e(nc.sbuf_tensor("A_sb", [128, DC, NST], F32))
        At_sb = e(nc.sbuf_tensor("At_sb", [128, DC * NST], F32))
        bdt_sb = e(nc.sbuf_tensor("bdt_sb", [128, DC], F32))
        cbv_sb = e(nc.sbuf_tensor("cbv_sb", [128, DC], F32))
        Dv_sb = e(nc.sbuf_tensor("Dv_sb", [128, DC], F32))
        cw_sb = e(nc.sbuf_tensor("cw_sb", [128, DC, 4], F32))
        eye_sb = e(nc.sbuf_tensor("eye_sb", [128, 128], BF16))
        eyef_sb = e(nc.sbuf_tensor("eyef_sb", [128, 128], F32))
        sel_sb = e(nc.sbuf_tensor("sel_sb", [128, 16 * 128], BF16))
        cst_sb = e(nc.sbuf_tensor("cst_sb", [128, 4], F32))
        onesf_sb = e(nc.sbuf_tensor("onesf_sb", [1, 128], F32))
        invD_sb = e(nc.sbuf_tensor("invD_sb", [128, 1], F32))
        carry_sb = e(nc.sbuf_tensor("carry_sb", [128, DC * NST], F32))
        pmax_sb = e(nc.sbuf_tensor("pmax_sb", [128, 2], F32))
        hmax_sb = e(nc.sbuf_tensor("hmax_sb", [128, 2], F32))
        sin1_sb = e(nc.sbuf_tensor("sin1_sb", [128, 1], F32))
        rs1_sb = e(nc.sbuf_tensor("rs1_sb", [128, 1], F32))
        sin2_sb = e(nc.sbuf_tensor("sin2_sb", [128, 1], F32))
        rs2_sb = e(nc.sbuf_tensor("rs2_sb", [128, 1], F32))
        sw1_sb = e(nc.sbuf_tensor("sw1_sb", [128, FC], F32))
        se1_sb = e(nc.sbuf_tensor("se1_sb", [128, FC], F32))
        bq1_sb = e(nc.sbuf_tensor("bq1_sb", [128, FC], F32))
        b1_sb = e(nc.sbuf_tensor("b1_sb", [128, FC], F32))
        sw2_sb = e(nc.sbuf_tensor("sw2_sb", [128, ECD], F32))
        se2_sb = e(nc.sbuf_tensor("se2_sb", [128, ECD], F32))
        bq2_sb = e(nc.sbuf_tensor("bq2_sb", [128, ECD], F32))
        b2_sb = e(nc.sbuf_tensor("b2_sb", [128, ECD], F32))
        tiny_sb = e(nc.sbuf_tensor("tiny_sb", [128, 16], F32))
        stat_sb = e(nc.sbuf_tensor("stat_sb", [1, 8 * 512], F32))

        consts = [1.0, EPSLN, 1.0 / 255.0, EPSQ]
        for i, v in enumerate(consts):
            nc.const_aps.register(F32, v, cst_sb[:, i : i + 1])

        psG = [e(nc.psum_tensor(f"psG{i}", [128, 512], F32)) for i in range(6)]
        psT_g = e(nc.psum_tensor("psT_g", [128, 128], BF16))
        psAR_g = e(nc.psum_tensor("psAR_g", [128, 128], F32))

        sems = {k: Ctr(e(nc.semaphore(k))) for k in ["dS", "dG", "tT", "sS", "vV", "CC"]}
        block = e(nc.Block())
        P = Prog(nc, block, sems)

        P.op("sync", lambda E: E.dma_start(out=eye_sb[:], in_=eye_i), "dS", 16)
        P.op("sync", lambda E: E.dma_start(out=sel_sb[0:16, :], in_=sel_i), "dS", 16)
        P.op("sync", lambda E: E.dma_start(out=sel_sb[32:48, :], in_=sel_i), "dS", 16)
        Aap = A_sb[:].rearrange("p c n -> p (c n)")
        P.op("gpsimd", lambda E: E.dma_start(
            out=At_sb[:].rearrange("p (c n) -> p c n", n=NST),
            in_=A_log_i.rearrange("(c p) n -> p c n", p=128)), "dG", 16)
        P.op("gpsimd", lambda E: E.dma_start(
            out=bdt_sb[:].rearrange("p (c o) -> p c o", o=1), in_=b_dt_i.rearrange("(c p) o -> p c o", p=128), allow_slow_non_contiguous=True), "dG", 16)
        P.op("gpsimd", lambda E: E.dma_start(
            out=cbv_sb[:].rearrange("p (c o) -> p c o", o=1), in_=conv_b_i.rearrange("(c p) o -> p c o", p=128), allow_slow_non_contiguous=True), "dG", 16)
        P.op("gpsimd", lambda E: E.dma_start(
            out=Dv_sb[:].rearrange("p (c o) -> p c o", o=1), in_=D_i.rearrange("(c p) o -> p c o", p=128), allow_slow_non_contiguous=True), "dG", 16)
        P.op("gpsimd", lambda E: E.dma_start(
            out=cw_sb[:],
            in_=conv_w_i.rearrange("(c p) k -> p c k", p=128)), "dG", 16)
        P.op("gpsimd", lambda E: E.dma_start(
            out=b1_sb[:].rearrange("p (c o) -> p c o", o=1), in_=b1_i.rearrange("(c p) o -> p c o", p=128), allow_slow_non_contiguous=True), "dG", 16)
        P.op("gpsimd", lambda E: E.dma_start(
            out=b2_sb[:].rearrange("p (c o) -> p c o", o=1), in_=b2_i.rearrange("(c p) o -> p c o", p=128), allow_slow_non_contiguous=True), "dG", 16)

        for i, v in enumerate(consts):
            P.op("vector", lambda E, i=i, v=v: E.memset(cst_sb[:, i : i + 1], v))
        P.op("vector", lambda E: E.memset(onesf_sb[:, :], 1.0))
        P.op("vector", lambda E: E.memset(invD_sb[:, :], 1.0 / 1024.0))
        P.op("vector", lambda E: E.memset(carry_sb[:, :], 0.0))
        P.op("vector", lambda E: E.memset(pmax_sb[:, :], -1e30))
        P.op("vector", lambda E: E.memset(hmax_sb[:, :], -1e30))
        P.wait_now("vector", "dS")
        cst_done = P.op("vector", lambda E: E.tensor_copy(eyef_sb[:], eye_sb[:]), "vV")

        P.wait_now("scalar", "dG")
        e1 = P.op("scalar", lambda E: E.activation(Aap, At_sb[:], AF.Exp, scale=1.0), "sS")
        P.wait("scalar", "sS", e1)
        e2 = P.op("scalar", lambda E: E.mul(At_sb[:], Aap, -1.0), "sS")
        P.wait("scalar", "sS", e2)
        P.op("scalar", lambda E: E.copy(Aap, At_sb[:]), "sS")

        P.barrier_all()

        xTv = xT.rearrange("(c p) t -> p c t", p=128)

        if active("M1"):
            with contextlib.ExitStack() as ctx:
                ee = ctx.enter_context
                Wi_sb = ee(nc.sbuf_tensor("Wi_sb", [128, ECD, 2 * DI], BF16))
                Wx_sb = ee(nc.sbuf_tensor("Wx_sb", [128, DC, 128], BF16))
                diag_sb = ee(nc.sbuf_tensor("diag_sb", [128, DC * 4, 128], BF16))
                xk_sb = ee(nc.sbuf_tensor("xk_sb", [128, ECD, L1], BF16))
                xip_sb = ee(nc.sbuf_tensor("xip_sb", [128, DC, L1 + 3], BF16))
                xcv_sb = ee(nc.sbuf_tensor("xcv_sb", [128, DC, L1], BF16))
                szt_sb = ee(nc.sbuf_tensor("szt_sb", [128, DC, L1], BF16))
                psA = [psG[0], psG[1]]
                psC = [psG[2], psG[3]]
                psX = psG[4]

                wload = P.op("gpsimd", lambda E: E.dma_start(
                    out=Wi_sb[:],
                    in_=W_inT.rearrange("(c p) m -> p c m", p=128)), "dG", 16)
                wload = P.op("gpsimd", lambda E: E.dma_start(
                    out=Wx_sb[:],
                    in_=W_xTp.rearrange("(c p) m -> p c m", p=128)), "dG", 16)

                for dc in range(DC):
                    for kk in range(4):
                        last_diag = P.op("vector", lambda E, dc=dc, kk=kk: E.tensor_scalar(
                            diag_sb[:, dc * 4 + kk, :], eye_sb[:],
                            cw_sb[:, dc, kk : kk + 1], None, ALU.mult), "vV")
                for dc in range(DC):
                    P.op("vector", lambda E, dc=dc: E.memset(xip_sb[:, dc, 0:3], 0.0))
                pad_done = P.op("vector", lambda E: E.memset(tiny_sb[:, 0:1], 0.0), "vV")

                evac_done = {}
                conv_evac = {}
                store_done = {}
                xz_mm = {}
                conv_mm = {}
                xload = {}

                for k in range(NT1):
                    ks = slice(k * L1, (k + 1) * L1)
                    P.barrier_all()
                    xload[k] = P.op("gpsimd", lambda E, ks=ks: E.dma_start(
                        out=xk_sb[:],
                        in_=xTv[:, :, ks]), "dG", 16)

                    P.wait("tensor", "dG", xload[k])
                    if k >= 1:
                        P.wait("scalar", "dS", store_done[k - 1])
                    for ec in range(2 * DC):
                        prev = (k, ec - 2) if ec >= 2 else ((k - 1, 2 * DC - 2 + ec) if k >= 1 else None)
                        if prev is not None:
                            P.wait("tensor", "sS", evac_done[prev])
                        for dk in range(ECD):
                            xz_mm[(k, ec)] = P.op(
                                "tensor",
                                lambda E, ec=ec, dk=dk: E.matmul(
                                    psA[ec % 2][:],
                                    Wi_sb[:, dk, ec * 128 : (ec + 1) * 128],
                                    xk_sb[:, dk, :],
                                    start=(dk == 0), stop=(dk == ECD - 1)),
                                "tT" if dk == ECD - 1 else None)
                        P.wait("scalar", "tT", xz_mm[(k, ec)])
                        if ec < DC:
                            dc = ec
                            if k >= 1:
                                P.wait("scalar", "tT", conv_mm[(k - 1, dc)])
                                cce = P.op("scalar", lambda E, dc=dc: E.copy(
                                    xip_sb[:, dc, 0:3], xip_sb[:, dc, L1 : L1 + 3]), "sS")
                                P.wait("scalar", "sS", cce)
                            evac_done[(k, ec)] = P.op(
                                "scalar", lambda E, dc=dc, ec=ec: E.copy(
                                    xip_sb[:, dc, 3 : L1 + 3], psA[ec % 2][:]), "sS")
                        else:
                            evac_done[(k, ec)] = P.op(
                                "scalar", lambda E, ec=ec: E.activation(
                                    szt_sb[:, ec - DC, :], psA[ec % 2][:], AF.Silu), "sS")

                    P.wait("tensor", "vV", pad_done)
                    for dc in range(DC):
                        P.wait("tensor", "sS", evac_done[(k, dc)])
                        prev = (k, dc - 2) if dc >= 2 else ((k - 1, DC - 2 + dc) if k >= 1 else None)
                        if prev is not None:
                            P.wait("tensor", "sS", conv_evac[prev])
                        for kk in range(4):
                            conv_mm[(k, dc)] = P.op(
                                "tensor", lambda E, dc=dc, kk=kk: E.matmul(
                                    psC[dc % 2][:], diag_sb[:, dc * 4 + kk, :],
                                    xip_sb[:, dc, kk : kk + L1],
                                    start=(kk == 0), stop=(kk == 3)),
                                "tT" if kk == 3 else None)
                        P.wait("scalar", "tT", conv_mm[(k, dc)])
                        conv_evac[(k, dc)] = P.op(
                            "scalar", lambda E, dc=dc: E.activation(
                                xcv_sb[:, dc, :], psC[dc % 2][:], AF.Silu,
                                bias=cbv_sb[:, dc : dc + 1]), "sS")

                    P.wait("tensor", "sS", conv_evac[(k, DC - 1)])
                    for kdc in range(DC):
                        xdbl_mm = P.op("tensor", lambda E, kdc=kdc: E.matmul(
                            psX[:], Wx_sb[:, kdc, :], xcv_sb[:, kdc, :],
                            start=(kdc == 0), stop=(kdc == DC - 1)),
                            "tT" if kdc == DC - 1 else None)
                    P.wait("scalar", "tT", xdbl_mm)
                    xdbl_evac = P.op("scalar", lambda E, ks=ks: E.copy(
                        xdbl_sb[:, ks], psX[:]), "sS")

                    P.wait("sync", "sS", conv_evac[(k, DC - 1)])
                    P.op("sync", lambda E, ks=ks: E.dma_start(
                        out=xconv_sp[:, :, ks],
                        in_=xcv_sb[:]), "dS", 16)
                    P.wait("sync", "sS", evac_done[(k, 2 * DC - 1)])
                    store_done[k] = P.op("sync", lambda E, ks=ks: E.dma_start(
                        out=sz_sp[:, :, ks],
                        in_=szt_sb[:]), "dS", 16)

                P.barrier_all()

        if active("M2"):
            with contextlib.ExitStack() as ctx:
                ee = ctx.enter_context
                Wdt_sb = ee(nc.sbuf_tensor("Wdt_sb", [128, DC, 128], BF16))
                xcv2_sb = ee(nc.sbuf_tensor("xcv2_sb", [128, DC, L1], BF16))
                szt2_sb = ee(nc.sbuf_tensor("szt2_sb", [128, DC, L1], BF16))
                dt_sb = ee(nc.sbuf_tensor("dt_sb", [128, DC, L1], F32))
                dte_sb = ee(nc.sbuf_tensor("dte_sb", [128, L1], F32))
                u_sb = ee(nc.sbuf_tensor("u_sb", [128, DC, L1], BF16))
                Bb_sb = ee(nc.sbuf_tensor("Bb_sb", [128, NST, L1], BF16))
                Cb_sb = ee(nc.sbuf_tensor("Cb_sb", [128, NST, L1], BF16))
                y2_sb = ee(nc.sbuf_tensor("y2_sb", [128, DC, L1], BF16))
                dA_sb = [ee(nc.sbuf_tensor(f"dA{i}", [128, L1], F32)) for i in range(2)]
                Zt_sb = [ee(nc.sbuf_tensor(f"Zt{i}", [128, L1], BF16)) for i in range(2)]
                st_sb = [ee(nc.sbuf_tensor(f"st{i}", [128, L1], BF16)) for i in range(2)]
                Pt_sb = [ee(nc.sbuf_tensor(f"Pt{i}", [128, L1], BF16)) for i in range(2)]
                tg_sb = ee(nc.sbuf_tensor("tg_sb", [128, L1], F32))
                psD = [psG[0], psG[1]]
                psBC = [psG[2], psG[3]]
                psY = [psG[4], psG[5]]

                P.op("gpsimd", lambda E: E.dma_start(
                    out=Wdt_sb[64:128, :, :],
                    in_=W_dtT_i.rearrange("k (c m) -> k c m", m=128)), "dG", 16)

                dt_ev = {}
                bc_ev = {}
                u_done = {}
                dA_done = {}
                scanP = {}
                ymm = {}
                gat = {}
                ld_done = {}
                st_done = {}

                for k in range(NT1):
                    ks = slice(k * L1, (k + 1) * L1)
                    P.barrier_all()
                    P.op("sync", lambda E, ks=ks: E.dma_start(
                        out=xcv2_sb[:],
                        in_=xconv_sp[:, :, ks]), "dS", 16)
                    ld_done[k] = P.op("sync", lambda E, ks=ks: E.dma_start(
                        out=szt2_sb[:],
                        in_=sz_sp[:, :, ks]), "dS", 16)

                    P.wait_now("tensor", "dG")
                    if k >= 1:
                        pass
                    for dc in range(DC):
                        if dc >= 2:
                            P.wait("tensor", "sS", dt_ev[(k, dc - 2)])
                        elif k >= 1:
                            P.wait("tensor", "sS", dt_ev[(k - 1, DC - 2 + dc)])
                        dtmm = P.op("tensor", lambda E, dc=dc, ks=ks: E.matmul(
                            psD[dc % 2][:], Wdt_sb[64:128, dc, :], xdbl_sb[64:128, ks],
                            start=True, stop=True), "tT")
                        P.wait("scalar", "tT", dtmm)
                        if k >= 1:
                            P.wait("scalar", "vV", u_done[(k - 1, dc)])
                        if (k, dc - 1) in dt_ev:
                            P.wait("scalar", "sS", dt_ev[(k, dc - 1)])
                        de = P.op("scalar", lambda E, dc=dc: E.activation(
                            dte_sb[:], psD[dc % 2][:], AF.Exp,
                            bias=bdt_sb[:, dc : dc + 1]), "sS")
                        P.wait("scalar", "sS", de)
                        dt_ev[(k, dc)] = P.op("scalar", lambda E, dc=dc: E.activation(
                            dt_sb[:, dc, :], dte_sb[:], AF.Ln, bias=1.0), "sS")

                    for n in range(NST):
                        for which, src_lo, selbase, dst in (
                                (0, 0, 0, Bb_sb), (1, 32, 32, Cb_sb)):
                            idx = n * 2 + which
                            if idx >= 2:
                                P.wait("tensor", "sS", bc_ev[(k, idx - 2)])
                            elif k >= 1:
                                P.wait("tensor", "sS", bc_ev[(k - 1, 2 * NST - 2 + idx)])
                            bcm = P.op("tensor", lambda E, n=n, src_lo=src_lo,
                                       selbase=selbase, ks=ks, idx=idx: E.matmul(
                                           psBC[idx % 2][:],
                                           sel_sb[selbase : selbase + 16,
                                                  n * 128 : (n + 1) * 128],
                                           xdbl_sb[src_lo : src_lo + 16, ks],
                                           start=True, stop=True), "tT")
                            P.wait("scalar", "tT", bcm)
                            bc_ev[(k, idx)] = P.op(
                                "scalar", lambda E, n=n, dst=dst, idx=idx: E.copy(
                                    dst[:, n, :], psBC[idx % 2][:]), "sS")

                    P.wait("vector", "dS", ld_done[k])
                    for dc in range(DC):
                        P.wait("vector", "sS", dt_ev[(k, dc)])
                        u_done[(k, dc)] = P.op("vector", lambda E, dc=dc: E.tensor_tensor(
                            u_sb[:, dc, :], dt_sb[:, dc, :], xcv2_sb[:, dc, :],
                            ALU.mult), "vV")

                    P.wait("vector", "sS", bc_ev[(k, 2 * NST - 1)])
                    if k >= 1:
                        P.wait("vector", "dS", st_done[k - 1])
                    step = 0
                    for dc in range(DC):
                        P.wait("vector", "vV", u_done[(k, dc)])
                        for n in range(NST):
                            i2 = step % 2
                            if step >= 2:
                                P.wait("scalar", "vV", scanP[step - 2])
                            dA_done[step] = P.op(
                                "scalar", lambda E, dc=dc, n=n, i2=i2: E.activation(
                                    dA_sb[i2][:], dt_sb[:, dc, :], AF.Exp,
                                    scale=A_sb[:, dc, n : n + 1]), "sS")
                            zi = P.op("vector", lambda E, dc=dc, n=n, i2=i2: E.tensor_tensor(
                                Zt_sb[i2][:], u_sb[:, dc, :], Bb_sb[:, n, :], ALU.mult), "vV")
                            P.wait("vector", "sS", dA_done[step])
                            P.wait("vector", "vV", zi)
                            cidx = dc * NST + n
                            if k == 0:
                                si = P.op("vector", lambda E, i2=i2: E.tensor_tensor_scan(
                                    st_sb[i2][:], dA_sb[i2][:], Zt_sb[i2][:],
                                    0.0, ALU.mult, ALU.add), "vV")
                            else:
                                si = P.op("vector", lambda E, i2=i2, cidx=cidx:
                                     E.tensor_tensor_scan(
                                         st_sb[i2][:], dA_sb[i2][:], Zt_sb[i2][:],
                                         carry_sb[:, cidx : cidx + 1],
                                         ALU.mult, ALU.add), "vV")
                            P.wait("vector", "vV", si)
                            P.op("vector", lambda E, i2=i2, cidx=cidx: E.tensor_copy(
                                carry_sb[:, cidx : cidx + 1], st_sb[i2][:, L1 - 1 : L1]), "vV")
                            if step >= 2:
                                P.wait("vector", "tT", ymm[step - 2])
                            scanP[step] = P.op(
                                "vector", lambda E, i2=i2, n=n: E.tensor_tensor(
                                    Pt_sb[i2][:], st_sb[i2][:], Cb_sb[:, n, :],
                                    ALU.mult), "vV")
                            P.wait("tensor", "vV", scanP[step])
                            if n == 0 and dc >= 2:
                                P.wait("tensor", "vV", gat[(k, dc - 2)])
                            elif n == 0 and k >= 1 and dc < 2:
                                P.wait("tensor", "vV", gat[(k - 1, DC - 2 + dc)])
                            ymm[step] = P.op("tensor", lambda E, i2=i2, dc=dc, n=n:
                                             E.matmul(psY[dc % 2][:], eye_sb[:],
                                                      Pt_sb[i2][:],
                                                      start=(n == 0), stop=(n == NST - 1)),
                                             "tT")
                            step += 1
                        P.wait("vector", "tT", ymm[step - 1])
                        tgi = P.op("vector", lambda E, dc=dc: E.tensor_tensor(
                            tg_sb[:], psY[dc % 2][:], szt2_sb[:, dc, :], ALU.mult), "vV")
                        P.wait("vector", "vV", tgi)
                        gat[(k, dc)] = P.op(
                            "vector", lambda E, dc=dc: E.scalar_tensor_tensor(
                                y2_sb[:, dc, :], xcv2_sb[:, dc, :],
                                Dv_sb[:, dc : dc + 1], tg_sb[:],
                                ALU.mult, ALU.add), "vV")

                    P.wait("sync", "vV", gat[(k, DC - 1)])
                    st_done[k] = P.op("sync", lambda E, ks=ks: E.dma_start(
                        out=y2_sp[:, :, ks],
                        in_=y2_sb[:]), "dS", 16)
                    if debug_spills:
                        P.wait("sync", "sS", dt_ev[(k, DC - 1)])
                        dbg_done = P.op("sync", lambda E, ks=ks: E.dma_start(
                            out=dbg_sp[:, :, ks],
                            in_=dt_sb[:]), "dS", 16)
                        st_done[k] = dbg_done
                        P.wait("scalar", "dS", dbg_done)

                P.barrier_all()

        def layer_norm_tiles(Ltile, v_sb, vsq_sb, x1out_sb, psm, psq, psB, necd=ECD):
            mean_s = stat_sb[:, 0:Ltile]
            msq_s = stat_sb[:, 512 : 512 + Ltile]
            var_s = stat_sb[:, 1024 : 1024 + Ltile]
            rstd_s = stat_sb[:, 1536 : 1536 + Ltile]
            mrstd_s = stat_sb[:, 2048 : 2048 + Ltile]
            tmp_s = stat_sb[:, 2560 : 2560 + Ltile]
            tmp2_s = stat_sb[:, 2560 + 512 : 2560 + 512 + Ltile]
            sq = P.op("scalar", lambda E: E.activation(
                vsq_sb[:].rearrange("p c l -> p (c l)"),
                v_sb[:].rearrange("p c l -> p (c l)"), AF.Square), "sS")
            P.wait("tensor", "sS", sq)
            for ec in range(necd):
                P.op("tensor", lambda E, ec=ec: E.matmul(
                    psm[0:1, :Ltile], invD_sb[:], v_sb[:, ec, :],
                    start=(ec == 0), stop=(ec == necd - 1)))
            for ec in range(necd):
                smm = P.op("tensor", lambda E, ec=ec: E.matmul(
                    psq[0:1, :Ltile], invD_sb[:], vsq_sb[:, ec, :],
                    start=(ec == 0), stop=(ec == necd - 1)),
                    "tT" if ec == necd - 1 else None)
            P.wait("scalar", "tT", smm)
            P.op("scalar", lambda E: E.copy(mean_s, psm[0:1, :Ltile]))
            ecop = P.op("scalar", lambda E: E.copy(msq_s, psq[0:1, :Ltile]), "sS")
            P.wait("vector", "sS", ecop)
            tmpi = P.op("vector", lambda E: E.tensor_tensor(
                tmp_s, mean_s, mean_s, ALU.mult), "vV")
            P.wait("vector", "vV", tmpi)
            vv = P.op("vector", lambda E: E.tensor_tensor(
                var_s, msq_s, tmp_s, ALU.subtract), "vV")
            P.wait("scalar", "vV", vv)
            lni = P.op("scalar", lambda E: E.activation(
                tmp2_s, var_s, AF.Ln, bias=EPSLN), "sS")
            P.wait("scalar", "sS", lni)
            rr = P.op("scalar", lambda E: E.activation(
                rstd_s, tmp2_s, AF.Exp, scale=-0.5), "sS")
            P.wait("vector", "sS", rr)
            mm = P.op("vector", lambda E: E.tensor_tensor(
                mrstd_s, mean_s, rstd_s, ALU.mult), "vV")
            P.wait("tensor", "vV", mm)
            P.op("tensor", lambda E: E.matmul(
                psB[0][0:128, :Ltile], onesf_sb[:], rstd_s, start=True, stop=True))
            bb = P.op("tensor", lambda E: E.matmul(
                psB[1][0:128, :Ltile], onesf_sb[:], mrstd_s, start=True, stop=True), "tT")
            P.wait("vector", "tT", bb)
            for ec in range(necd):
                mi = P.op("vector", lambda E, ec=ec: E.tensor_tensor(
                    vsq_sb[:, ec, :], v_sb[:, ec, :], psB[0][0:128, :Ltile], ALU.mult), "vV")
                P.wait("vector", "vV", mi)
                lnv = P.op("vector", lambda E, ec=ec: E.tensor_tensor(
                    x1out_sb[:, ec, :], vsq_sb[:, ec, :], psB[1][0:128, :Ltile],
                    ALU.subtract), "vV")
            return lnv

        if active("M3"):
            with contextlib.ExitStack() as ctx:
                ee = ctx.enter_context
                Wo_sb = ee(nc.sbuf_tensor("Wo_sb", [128, DC, D], BF16))
                y2l_sb = ee(nc.sbuf_tensor("y2l_sb", [128, DC, L1], BF16))
                xf_sb = ee(nc.sbuf_tensor("xf_sb", [128, ECD, L1], F32))
                v_sb = ee(nc.sbuf_tensor("v_sb", [128, ECD, L1], F32))
                vsq_sb = ee(nc.sbuf_tensor("vsq_sb", [128, ECD, L1], F32))
                x1_sb = ee(nc.sbuf_tensor("x1_sb", [128, ECD, L1], F32))
                psO = [psG[0], psG[1]]
                psm = psG[2]
                psq = psG[3]
                psB = [psG[4], psG[5]]

                P.op("gpsimd", lambda E: E.dma_start(
                    out=Wo_sb[:],
                    in_=W_outT_i.rearrange("(c p) m -> p c m", p=128)), "dG", 16)

                stx = None
                for k in range(NT1):
                    ks = slice(k * L1, (k + 1) * L1)
                    P.barrier_all()
                    P.op("sync", lambda E, ks=ks: E.dma_start(
                        out=y2l_sb[:],
                        in_=y2_sp[:, :, ks]), "dS", 16)
                    ldk = P.op("sync", lambda E, ks=ks: E.dma_start(
                        out=xf_sb[:],
                        in_=xTv[:, :, ks]), "dS", 16)
                    P.wait("tensor", "dS", ldk)
                    P.wait_now("tensor", "dG")
                    vadds = {}
                    for ec in range(ECD):
                        if ec >= 2:
                            P.wait("tensor", "vV", vadds[ec - 2])
                        for kdc in range(DC):
                            omm = P.op("tensor", lambda E, ec=ec, kdc=kdc: E.matmul(
                                psO[ec % 2][:], Wo_sb[:, kdc, ec * 128 : (ec + 1) * 128],
                                y2l_sb[:, kdc, :],
                                start=(kdc == 0), stop=(kdc == DC - 1)),
                                "tT" if kdc == DC - 1 else None)
                        P.wait("vector", "tT", omm)
                        vadds[ec] = P.op("vector", lambda E, ec=ec: E.tensor_tensor(
                            v_sb[:, ec, :], psO[ec % 2][:], xf_sb[:, ec, :],
                            ALU.add), "vV")
                    P.wait("scalar", "vV", vadds[ECD - 1])
                    P.wait("tensor", "vV", vadds[ECD - 1])
                    lnv = layer_norm_tiles(L1, v_sb, vsq_sb, x1_sb, psm, psq, psB)
                    P.wait("vector", "vV", lnv)
                    ri = P.op("vector", lambda E: E.tensor_reduce(
                        tiny_sb[:, 0:1], x1_sb[:].rearrange("p c l -> p (c l)"),
                        AX.X, ALU.max), "vV")
                    P.wait("vector", "vV", ri)
                    a, b = k % 2, (k + 1) % 2
                    pm = P.op("vector", lambda E, a=a, b=b: E.tensor_tensor(
                        pmax_sb[:, b : b + 1], pmax_sb[:, a : a + 1],
                        tiny_sb[:, 0:1], ALU.max), "vV")
                    P.wait("sync", "vV", lnv)
                    stx = P.op("sync", lambda E, ks=ks: E.dma_start(
                        out=x1_sp[:, :, ks],
                        in_=x1_sb[:]), "dS", 16)
                    P.wait("sync", "vV", pm)
                    P.wait("sync", "dS", stx)

                P.barrier_all()

        def allreduce_max(maxv_sb, sinv_sb, rsv_sb, ccin, ccout, psB0):
            tr = P.op("tensor", lambda E: E.transpose(
                psB0[0:1, 0:128], maxv_sb[:], eyef_sb[:]), "tT")
            P.wait("vector", "tT", tr)
            e1 = P.op("vector", lambda E: E.reduce_max(
                tiny_sb[0:1, 0:1], psB0[0:1, 0:128], AX.X), "vV")
            P.wait("vector", "vV", e1)
            gm = P.op("vector", lambda E: E.tensor_scalar(
                tiny_sb[0:1, 1:2], tiny_sb[0:1, 0:1], EPSQ, 1.0 / 255.0,
                ALU.max, ALU.mult), "vV")
            P.wait("tensor", "vV", gm)
            bc = P.op("tensor", lambda E: E.matmul(
                psB0[0:128, 0:1], onesf_sb[:], tiny_sb[0:1, 1:2],
                start=True, stop=True), "tT")
            P.wait("scalar", "tT", bc)
            sc = P.op("scalar", lambda E: E.copy(sinv_sb[:], psB0[0:128, 0:1]), "sS")
            P.wait("vector", "sS", sc)
            b16, _ = bass.broadcast_tensor_aps(sinv_sb[:, 0:1], tiny_sb[:, 0:16])
            vb = P.op("vector", lambda E, b16=b16: E.tensor_copy(
                tiny_sb[:, 0:16], b16), "vV")
            P.wait("gpsimd", "vV", vb)
            dg1 = P.op("gpsimd", lambda E: E.dma_start(
                out=ccin, in_=tiny_sb[:, 0:16]), "dG", 16)
            P.wait("gpsimd", "dG", dg1)
            cc = P.op("gpsimd", lambda E: E.collective_compute(
                "AllReduce", ALU.max,
                replica_groups=[[0, 1, 2, 3, 4, 5, 6, 7]],
                ins=[ccin], outs=[ccout]), "CC")
            P.wait("gpsimd", "CC", cc)
            dg2 = P.op("gpsimd", lambda E: E.dma_start(
                out=tiny_sb[:, 0:16], in_=ccout), "dG", 16)
            P.wait("vector", "dG", dg2)
            e2 = P.op("vector", lambda E: E.tensor_copy(sinv_sb[:], tiny_sb[:, 0:1]), "vV")
            P.wait("vector", "vV", e2)
            rv = P.op("vector", lambda E: E.reciprocal(rsv_sb[:], sinv_sb[:]), "vV")
            return rv

        if active("AR1"):
            allreduce_max(pmax_sb[:, 0:1], sin1_sb, rs1_sb, cc1_in, cc1_out, psAR_g)
            P.barrier_all()

        def wq_phase(win, nchunk, width, sw_sb, wi_sb, nd_out, b_sb, se_sb, bq_sb,
                     sin_sb, name):
            with contextlib.ExitStack() as ctx:
                ee = ctx.enter_context
                wch = ee(nc.sbuf_tensor(f"wch_{name}", [128, width], F32))
                q32 = ee(nc.sbuf_tensor(f"q32_{name}", [128, width], I32))
                qbf = ee(nc.sbuf_tensor(f"qbf_{name}", [128, width], BF16))
                rw = ee(nc.sbuf_tensor(f"rw_{name}", [128, 1], F32))
                tq = ee(nc.sbuf_tensor(f"tq_{name}", [128, width], F32))
                q32b = ee(nc.sbuf_tensor(f"q32b_{name}", [128, nchunk], I32))
                tmpb = ee(nc.sbuf_tensor(f"tmpb_{name}", [128, nchunk], F32))
                tmpb2 = ee(nc.sbuf_tensor(f"tmpb2_{name}", [128, nchunk], F32))
                psT = psT_g
                winv = win.rearrange("(c p) d -> p c d", p=128)
                ndc = width // 128
                for fcx in range(nchunk):
                    P.barrier_all()
                    ld = P.op("sync", lambda E, fcx=fcx: E.dma_start(
                        out=wch[:], in_=winv[:, fcx, :]), "dS", 16)
                    P.wait("vector", "dS", ld)
                    if fcx >= 1:
                        P.wait("vector", "sS", trev)
                    e1 = P.op("vector", lambda E, fcx=fcx: E.tensor_reduce(
                        tiny_sb[:, 4:5], wch[:], AX.X, ALU.max,
                        apply_absolute_value=True), "vV")
                    P.wait("vector", "vV", e1)
                    e2 = P.op("vector", lambda E, fcx=fcx: E.tensor_scalar(
                        sw_sb[:, fcx : fcx + 1], tiny_sb[:, 4:5],
                        EPSQ, 1.0 / 127.0, ALU.max, ALU.mult), "vV")
                    P.wait("vector", "vV", e2)
                    e3 = P.op("vector", lambda E, fcx=fcx: E.reciprocal(
                        rw[:], sw_sb[:, fcx : fcx + 1]), "vV")
                    P.wait("vector", "vV", e3)
                    e4 = P.op("vector", lambda E: E.tensor_scalar(
                        tq[:], wch[:], rw[:, 0:1], None, ALU.mult), "vV")
                    P.wait("vector", "vV", e4)
                    e4c = P.op("vector", lambda E: E.tensor_copy(
                        q32[:], tq[:]), "vV")
                    P.wait("vector", "vV", e4c)
                    qdone = P.op("vector", lambda E: E.tensor_scalar(
                        qbf[:], q32[:], -128.0, 127.0, ALU.max, ALU.min), "vV")
                    P.wait("tensor", "vV", qdone)
                    for dcc in range(ndc):
                        if dcc >= 1:
                            P.wait("tensor", "sS", trev)
                        trm = P.op("tensor", lambda E, dcc=dcc: E.transpose(
                            psT[:], qbf[:, dcc * 128 : (dcc + 1) * 128],
                            eye_sb[:]), "tT")
                        P.wait("scalar", "tT", trm)
                        trev = P.op("scalar", lambda E, dcc=dcc, fcx=fcx: E.copy(
                            wi_sb[:, dcc, fcx * 128 : (fcx + 1) * 128], psT[:]), "sS")
                e1 = P.op("vector", lambda E: E.tensor_scalar(
                    se_sb[:], sw_sb[:], sin_sb[:, 0:1], None, ALU.mult), "vV")
                P.wait("vector", "vV", e1)
                e2 = P.op("vector", lambda E: E.reciprocal(tmpb[:], se_sb[:]), "vV")
                P.wait("vector", "vV", e2)
                e3 = P.op("vector", lambda E: E.tensor_tensor(
                    tmpb2[:], b_sb[:], tmpb[:], ALU.mult), "vV")
                P.wait("vector", "vV", e3)
                e4 = P.op("vector", lambda E: E.tensor_copy(
                    q32b[:], tmpb2[:]), "vV")
                P.wait("vector", "vV", e4)
                bqd = P.op("vector", lambda E: E.tensor_tensor(
                    bq_sb[:], q32b[:], se_sb[:], ALU.mult), "vV")
                return bqd

        if active("WQ1"):
            wq1_ctx = top.enter_context(contextlib.ExitStack())
            w1i_sb = wq1_ctx.enter_context(
                nc.sbuf_tensor("w1i_sb", [128, ECD, FF], BF16))
            wq_phase(w1n_i, FC, D, sw1_sb, w1i_sb, ECD, b1_sb, se1_sb, bq1_sb,
                     sin1_sb, "w1")
            P.barrier_all()
            if debug_spills:
                P.op("gpsimd", lambda E: E.dma_start(out=w1i_dbg, in_=w1i_sb[:]), "dG", 16)
                P.op("gpsimd", lambda E: E.dma_start(out=q1_dbg[:, 0:FC], in_=sw1_sb[:]), "dG", 16)
                P.op("gpsimd", lambda E: E.dma_start(out=q1_dbg[:, FC:2 * FC], in_=se1_sb[:]), "dG", 16)
                P.op("gpsimd", lambda E: E.dma_start(out=q1_dbg[:, 2 * FC:3 * FC], in_=bq1_sb[:]), "dG", 16)
                P.op("gpsimd", lambda E: E.dma_start(out=q1_dbg[:, 3 * FC:3 * FC + 1], in_=sin1_sb[:], allow_slow_non_contiguous=True), "dG", 16)
                P.wait_now("gpsimd", "dG")
                P.barrier_all()

        if active("F1"):
            with contextlib.ExitStack() as ctx:
                ee = ctx.enter_context
                x1l_sb = ee(nc.sbuf_tensor("x1l_sb", [128, ECD, L1], F32))
                xi32_sb = ee(nc.sbuf_tensor("xi32_sb", [128, L1], I32))
                xi_sb = ee(nc.sbuf_tensor("xi_sb", [128, ECD, L1], BF16))
                h_sb = ee(nc.sbuf_tensor("h_sb", [128, FC, L1], F32))
                psH = [psG[0], psG[1]]

                hev = {}
                for k in range(NT1):
                    ks = slice(k * L1, (k + 1) * L1)
                    P.barrier_all()
                    ld = P.op("sync", lambda E, ks=ks: E.dma_start(
                        out=x1l_sb[:],
                        in_=x1_sp[:, :, ks]), "dS", 16)
                    P.wait("vector", "dS", ld)
                    for ec in range(ECD):
                        if k >= 1 and ec == 0:
                            P.wait("vector", "tT", hmm_last)
                        if ec >= 1:
                            P.wait("vector", "vV", qd)
                        q32i = P.op("vector", lambda E, ec=ec: E.tensor_scalar(
                            xi32_sb[:], x1l_sb[:, ec, :], rs1_sb[:, 0:1], None,
                            ALU.mult), "vV")
                        P.wait("vector", "vV", q32i)
                        qd = P.op("vector", lambda E, ec=ec: E.tensor_scalar(
                            xi_sb[:, ec, :], xi32_sb[:], 0.0, 255.0,
                            ALU.max, ALU.min), "vV")
                    P.wait("tensor", "vV", qd)
                    if debug_spills:
                        P.wait("gpsimd", "vV", qd)
                        P.op("gpsimd", lambda E, ks=ks: E.dma_start(
                            out=xi_dbg[:, :, ks], in_=xi_sb[:]), "dG", 16)
                    for fcx in range(FC):
                        prev = fcx - 2 if fcx >= 2 else (FC - 2 + fcx if k >= 1 else None)
                        if prev is not None and (k if fcx >= 2 else k - 1, prev) in hev:
                            P.wait("tensor", "sS",
                                   hev[(k if fcx >= 2 else k - 1, prev)])
                        for kd in range(ECD):
                            hmm_last = P.op("tensor", lambda E, fcx=fcx, kd=kd: E.matmul(
                                psH[fcx % 2][:],
                                w1i_sb[:, kd, fcx * 128 : (fcx + 1) * 128],
                                xi_sb[:, kd, :],
                                start=(kd == 0), stop=(kd == ECD - 1)),
                                "tT" if kd == ECD - 1 else None)
                        P.wait("scalar", "tT", hmm_last)
                        if k >= 1 and fcx == 0:
                            P.wait("scalar", "dS", hst)
                            P.wait("scalar", "vV", hm)
                        hev[(k, fcx)] = P.op("scalar", lambda E, fcx=fcx: E.activation(
                            h_sb[:, fcx, :], psH[fcx % 2][:], AF.Relu,
                            scale=se1_sb[:, fcx : fcx + 1],
                            bias=bq1_sb[:, fcx : fcx + 1]), "sS")
                    P.wait("vector", "sS", hev[(k, FC - 1)])
                    ri = P.op("vector", lambda E: E.tensor_reduce(
                        tiny_sb[:, 2:3], h_sb[:].rearrange("p c l -> p (c l)"),
                        AX.X, ALU.max), "vV")
                    P.wait("vector", "vV", ri)
                    a, b = k % 2, (k + 1) % 2
                    hm = P.op("vector", lambda E, a=a, b=b: E.tensor_tensor(
                        hmax_sb[:, b : b + 1], hmax_sb[:, a : a + 1],
                        tiny_sb[:, 2:3], ALU.max), "vV")
                    P.wait("sync", "sS", hev[(k, FC - 1)])
                    hst = P.op("sync", lambda E, ks=ks: E.dma_start(
                        out=h_sp[:, :, ks],
                        in_=h_sb[:]), "dS", 16)
                    P.wait("sync", "vV", hm)

                P.barrier_all()
            wq1_ctx.close()

        if active("AR2"):
            allreduce_max(hmax_sb[:, 0:1], sin2_sb, rs2_sb, cc2_in, cc2_out, psAR_g)
            P.barrier_all()

        if active("WQ2"):
            wq2_ctx = top.enter_context(contextlib.ExitStack())
            w2i_sb = wq2_ctx.enter_context(
                nc.sbuf_tensor("w2i_sb", [128, FC, D], BF16))
            wq_phase(w2n_i, ECD, FF, sw2_sb, w2i_sb, FC, b2_sb, se2_sb, bq2_sb,
                     sin2_sb, "w2")
            P.barrier_all()

        if active("F2"):
            with contextlib.ExitStack() as ctx:
                ee = ctx.enter_context
                h2_sb = ee(nc.sbuf_tensor("h2_sb", [128, FC, L2], F32))
                hi32_sb = ee(nc.sbuf_tensor("hi32_sb", [128, L2], I32))
                hi_sb = ee(nc.sbuf_tensor("hi_sb", [128, FC, L2], BF16))
                x12_sb = ee(nc.sbuf_tensor("x12_sb", [128, ECD, L2], F32))
                ffn_sb = ee(nc.sbuf_tensor("ffn_sb", [128, ECD, L2], F32))
                v2_sb = ee(nc.sbuf_tensor("v2_sb", [128, ECD, L2], F32))
                vsq2_sb = ee(nc.sbuf_tensor("vsq2_sb", [128, ECD, L2], F32))
                x2_sb = ee(nc.sbuf_tensor("x2_sb", [128, ECD, L2], F32))
                psO2 = [psG[0][:, 0:L2], psG[1][:, 0:L2]]
                psm2 = psG[2]
                psq2 = psG[3]
                psB2 = [psG[4], psG[5]]

                for k in range(NT2):
                    ks = slice(k * L2, (k + 1) * L2)
                    P.barrier_all()
                    P.op("sync", lambda E, ks=ks: E.dma_start(
                        out=h2_sb[:],
                        in_=h_sp[:, :, ks]), "dS", 16)
                    ld2 = P.op("sync", lambda E, ks=ks: E.dma_start(
                        out=x12_sb[:],
                        in_=x1_sp[:, :, ks]), "dS", 16)
                    P.wait("vector", "dS", ld2)
                    for fcx in range(FC):
                        if k >= 1 and fcx == 0:
                            P.wait("vector", "tT", omm2)
                        if fcx >= 1:
                            P.wait("vector", "vV", qd2)
                        q32i = P.op("vector", lambda E, fcx=fcx: E.tensor_scalar(
                            hi32_sb[:], h2_sb[:, fcx, :], rs2_sb[:, 0:1], None,
                            ALU.mult), "vV")
                        P.wait("vector", "vV", q32i)
                        qd2 = P.op("vector", lambda E, fcx=fcx: E.tensor_scalar(
                            hi_sb[:, fcx, :], hi32_sb[:], 0.0, 255.0,
                            ALU.max, ALU.min), "vV")
                    P.wait("tensor", "vV", qd2)
                    ev2 = {}
                    for ec in range(ECD):
                        if ec >= 2:
                            P.wait("tensor", "sS", ev2[ec - 2])
                        for kf in range(FC):
                            omm2 = P.op("tensor", lambda E, ec=ec, kf=kf: E.matmul(
                                psO2[ec % 2],
                                w2i_sb[:, kf, ec * 128 : (ec + 1) * 128],
                                hi_sb[:, kf, :],
                                start=(kf == 0), stop=(kf == FC - 1)),
                                "tT" if kf == FC - 1 else None)
                        P.wait("scalar", "tT", omm2)
                        ev2[ec] = P.op("scalar", lambda E, ec=ec: E.activation(
                            ffn_sb[:, ec, :], psO2[ec % 2], AF.Identity,
                            scale=se2_sb[:, ec : ec + 1],
                            bias=bq2_sb[:, ec : ec + 1]), "sS")
                    for ec in range(ECD):
                        P.wait("vector", "sS", ev2[ec])
                        va2 = P.op("vector", lambda E, ec=ec: E.tensor_tensor(
                            v2_sb[:, ec, :], ffn_sb[:, ec, :], x12_sb[:, ec, :],
                            ALU.add), "vV")
                    P.wait("scalar", "vV", va2)
                    P.wait("tensor", "vV", va2)
                    lnv2 = layer_norm_tiles(L2, v2_sb, vsq2_sb, x2_sb, psm2, psq2, psB2)
                    P.wait("sync", "vV", lnv2)
                    stx2 = P.op("sync", lambda E, ks=ks: E.dma_start(
                        out=outT.rearrange("(c p) t -> p c t", p=128)[:, :, ks],
                        in_=x2_sb[:]), "dS", 16)

                P.barrier_all()

        if debug_spills:
            P.barrier_all()
            P.op("gpsimd", lambda E: E.dma_start(out=xconv_dbg, in_=xconv_sp), "dG", 16)
            P.op("gpsimd", lambda E: E.dma_start(out=sz_dbg, in_=sz_sp), "dG", 16)
            if active("M2"):
                P.op("gpsimd", lambda E: E.dma_start(out=y2_dbg, in_=y2_sp), "dG", 16)

        fin = P.snap()
        P.wait_snap("sync", fin)

        P.flush()

    return nc



def _prep_shared(inputs):
    f32 = np.float32
    W_in = np.asarray(inputs["L0_attn_W_in"], f32)
    W_x = np.asarray(inputs["L0_attn_W_x"], f32)
    W_dt = np.asarray(inputs["L0_attn_W_dt"], f32)
    W_out = np.asarray(inputs["L0_attn_W_out"], f32)
    conv_w = np.asarray(inputs["L0_attn_conv_w"], f32).reshape(DI, 4)
    Wxp = np.zeros((128, DI), f32)
    Wxp[0:16] = W_x[RK : RK + NST]
    Wxp[32:48] = W_x[RK + NST : RK + 2 * NST]
    Wxp[64:128] = W_x[0:RK]
    sel = np.zeros((16, 16 * 128), dtype=ml_dtypes.bfloat16)
    for n in range(16):
        sel[n, n * 128 : (n + 1) * 128] = 1
    shared = {
        "W_inT": np.ascontiguousarray(W_in.T),
        "W_xTp": np.ascontiguousarray(Wxp.T),
        "W_dtT": np.ascontiguousarray(W_dt.T),
        "W_outT": np.ascontiguousarray(W_out.T),
        "w1n": np.ascontiguousarray(np.asarray(inputs["L0_ffn_w1"], f32)),
        "w2n": np.ascontiguousarray(np.asarray(inputs["L0_ffn_w2"], f32)),
        "conv_w": np.ascontiguousarray(conv_w),
        "conv_b": np.asarray(inputs["L0_attn_conv_b"], f32).reshape(DI, 1),
        "b_dt": np.asarray(inputs["L0_attn_b_dt"], f32).reshape(DI, 1),
        "A_log": np.ascontiguousarray(np.asarray(inputs["L0_attn_A_log"], f32)),
        "D_vec": np.asarray(inputs["L0_attn_D"], f32).reshape(DI, 1),
        "b1": np.asarray(inputs["L0_ffn_b1"], f32).reshape(FF, 1),
        "b2": np.asarray(inputs["L0_ffn_b2"], f32).reshape(D, 1),
        "eye": np.eye(128, dtype=ml_dtypes.bfloat16),
        "sel": sel,
    }
    return shared


def kernel(**inputs):
    x = np.asarray(inputs["L0_attn_x"], np.float32)
    shared = _prep_shared(inputs)
    nc = build_program()
    in_maps = []
    for b in range(8):
        m = dict(shared)
        m["xT"] = np.ascontiguousarray(x[b].T)
        in_maps.append(m)
    res = run_bass_kernel_spmd(nc, in_maps, core_ids=list(range(8)))
    out = np.stack([np.ascontiguousarray(res.results[b]["outT"].T) for b in range(8)])
    return out.astype(np.float32)


# version 07
# speedup vs baseline: 1.0395x; 1.0395x over previous
import contextlib

import numpy as np
import ml_dtypes

import concourse.bass as bass
import concourse.mybir as mybir
from concourse.bass_utils import run_bass_kernel_spmd

F32 = mybir.dt.float32
BF16 = mybir.dt.bfloat16
I32 = mybir.dt.int32
AF = mybir.ActivationFunctionType
ALU = mybir.AluOpType
AX = mybir.AxisListType

T = 4096
D = 1024
DI = 2048
NST = 16
RK = 64
FF = 4096
ECD = D // 128
DC = DI // 128
FC = FF // 128
L1 = 512
NT1 = T // L1
L2 = 256
NT2 = T // L2

EPSQ = 1e-8
EPSLN = 1e-5

PHASES = ["P0", "M1", "M2", "M3", "AR1", "WQ1", "F1", "AR2", "WQ2", "F2"]
ENGINES = ["sync", "gpsimd", "vector", "scalar", "tensor"]


class Ctr:
    def __init__(self, handle):
        self.h = handle
        self.v = 0


class Prog:

    def __init__(self, nc, block, sems):
        self.nc = nc
        self.block = block
        self.sems = sems
        self.streams = {n: [] for n in ENGINES}

    def op(self, eng, fn, inc=None, n=1):
        if inc is not None:
            ctr = self.sems[inc]
            ctr.v += n
            nn = n

            def thunk(E, fn=fn, ctr=ctr, nn=nn):
                fn(E).then_inc(ctr.h, nn)
        else:
            def thunk(E, fn=fn):
                fn(E)
        self.streams[eng].append(thunk)
        return self.sems[inc].v if inc is not None else None

    def wait(self, eng, sem, val):
        if val <= 0:
            return
        h = self.sems[sem].h
        self.streams[eng].append(lambda E, h=h, val=val: E.wait_ge(h, val))

    def wait_now(self, eng, sem):
        self.wait(eng, sem, self.sems[sem].v)

    def snap(self):
        return {k: c.v for k, c in self.sems.items()}

    def wait_snap(self, eng, snapshot):
        for k, v in snapshot.items():
            self.wait(eng, k, v)

    def barrier_all(self):
        s = self.snap()
        for eng in ENGINES:
            self.wait_snap(eng, s)

    def flush(self):
        for name in ENGINES:
            st = self.streams[name]
            self.streams[name] = []
            if st:
                getattr(self.block, name)(lambda E, st=st: [f(E) for f in st])


def build_program(phase_limit="F2", debug_spills=False, zpool=False):
    nc = bass.Bass("TRN2", target_bir_lowering=False)
    nlim = PHASES.index(phase_limit)

    def active(ph):
        return PHASES.index(ph) <= nlim

    def din(name, shape, dtype=F32):
        return nc.dram_tensor(name, shape, dtype, kind="ExternalInput").ap()

    xT = din("xT", [D, T])
    W_inT = din("W_inT", [D, 2 * DI])
    W_xTp = din("W_xTp", [DI, 128])
    W_dtT_i = din("W_dtT", [RK, DI])
    W_outT_i = din("W_outT", [DI, D])
    w1n_i = din("w1n", [FF, D])
    w2n_i = din("w2n", [D, FF])
    conv_w_i = din("conv_w", [DI, 4])
    conv_b_i = din("conv_b", [DI, 1])
    b_dt_i = din("b_dt", [DI, 1])
    A_log_i = din("A_log", [DI, NST])
    D_i = din("D_vec", [DI, 1])
    b1_i = din("b1", [FF, 1])
    b2_i = din("b2", [D, 1])
    eye_i = din("eye", [128, 128], BF16)
    sel_i = din("sel", [16, 16 * 128], BF16)

    outT = nc.dram_tensor("outT", [D, T], F32, kind="ExternalOutput").ap()

    spk = "ExternalOutput" if debug_spills else "Internal"
    xconv_sp = nc.dram_tensor("xconv_sp", [128, DC, T], BF16, kind="Internal").ap()
    sz_sp = nc.dram_tensor("sz_sp", [128, DC, T], BF16, kind="Internal").ap()
    y2_sp = nc.dram_tensor("y2_sp", [128, DC, T], BF16, kind="Internal").ap()
    x1_sp = nc.dram_tensor("x1_sp", [128, ECD, T], F32, kind=spk).ap()
    h_sp = nc.dram_tensor("h_sp", [128, FC, T], F32, kind=spk).ap()
    dbg_sp = nc.dram_tensor("dbg_sp", [128, DC, T], F32, kind=spk).ap()
    if debug_spills:
        xconv_dbg = nc.dram_tensor("xconv_dbg", [128, DC, T], F32, kind=spk).ap()
        sz_dbg = nc.dram_tensor("sz_dbg", [128, DC, T], F32, kind=spk).ap()
        y2_dbg = nc.dram_tensor("y2_dbg", [128, DC, T], F32, kind=spk).ap()
        w1i_dbg = nc.dram_tensor("w1i_dbg", [128, ECD, FF], F32, kind=spk).ap()
        xi_dbg = nc.dram_tensor("xi_dbg", [128, ECD, T], F32, kind=spk).ap()
        q1_dbg = nc.dram_tensor("q1_dbg", [128, 4 * FC], F32, kind=spk).ap()

    cc1_in = nc.dram_tensor("cc1_in", [128, 16], F32, kind="Internal").ap()
    cc1_out = nc.dram_tensor("cc1_out", [128, 16], F32, kind="Internal").ap()
    cc2_in = nc.dram_tensor("cc2_in", [128, 16], F32, kind="Internal").ap()
    cc2_out = nc.dram_tensor("cc2_out", [128, 16], F32, kind="Internal").ap()

    with contextlib.ExitStack() as top:
        e = top.enter_context

        xdbl_sb = e(nc.sbuf_tensor("xdbl_sb", [128, T], BF16))
        A_sb = e(nc.sbuf_tensor("A_sb", [128, DC, NST], F32))
        At_sb = e(nc.sbuf_tensor("At_sb", [128, DC * NST], F32))
        bdt_sb = e(nc.sbuf_tensor("bdt_sb", [128, DC], F32))
        cbv_sb = e(nc.sbuf_tensor("cbv_sb", [128, DC], F32))
        Dv_sb = e(nc.sbuf_tensor("Dv_sb", [128, DC], F32))
        cw_sb = e(nc.sbuf_tensor("cw_sb", [128, DC, 4], F32))
        eye_sb = e(nc.sbuf_tensor("eye_sb", [128, 128], BF16))
        eyef_sb = e(nc.sbuf_tensor("eyef_sb", [128, 128], F32))
        sel_sb = e(nc.sbuf_tensor("sel_sb", [128, 16 * 128], BF16))
        cst_sb = e(nc.sbuf_tensor("cst_sb", [128, 4], F32))
        onesf_sb = e(nc.sbuf_tensor("onesf_sb", [1, 128], F32))
        invD_sb = e(nc.sbuf_tensor("invD_sb", [128, 1], F32))
        carry_sb = e(nc.sbuf_tensor("carry_sb", [128, DC * NST], F32))
        pmax_sb = e(nc.sbuf_tensor("pmax_sb", [128, 2], F32))
        hmax_sb = e(nc.sbuf_tensor("hmax_sb", [128, 2], F32))
        sin1_sb = e(nc.sbuf_tensor("sin1_sb", [128, 1], F32))
        rs1_sb = e(nc.sbuf_tensor("rs1_sb", [128, 1], F32))
        sin2_sb = e(nc.sbuf_tensor("sin2_sb", [128, 1], F32))
        rs2_sb = e(nc.sbuf_tensor("rs2_sb", [128, 1], F32))
        sw1_sb = e(nc.sbuf_tensor("sw1_sb", [128, FC], F32))
        se1_sb = e(nc.sbuf_tensor("se1_sb", [128, FC], F32))
        bq1_sb = e(nc.sbuf_tensor("bq1_sb", [128, FC], F32))
        b1_sb = e(nc.sbuf_tensor("b1_sb", [128, FC], F32))
        sw2_sb = e(nc.sbuf_tensor("sw2_sb", [128, ECD], F32))
        se2_sb = e(nc.sbuf_tensor("se2_sb", [128, ECD], F32))
        bq2_sb = e(nc.sbuf_tensor("bq2_sb", [128, ECD], F32))
        b2_sb = e(nc.sbuf_tensor("b2_sb", [128, ECD], F32))
        tiny_sb = e(nc.sbuf_tensor("tiny_sb", [128, 16], F32))
        stat_sb = e(nc.sbuf_tensor("stat_sb", [1, 8 * 512], F32))

        consts = [1.0, EPSLN, 1.0 / 255.0, EPSQ]
        for i, v in enumerate(consts):
            nc.const_aps.register(F32, v, cst_sb[:, i : i + 1])

        psG = [e(nc.psum_tensor(f"psG{i}", [128, 512], F32)) for i in range(6)]
        psT_g = e(nc.psum_tensor("psT_g", [128, 128], BF16))
        psAR_g = e(nc.psum_tensor("psAR_g", [128, 128], F32))

        sems = {k: Ctr(e(nc.semaphore(k))) for k in ["dS", "dG", "tT", "sS", "vV", "CC", "pP"]}
        block = e(nc.Block())
        P = Prog(nc, block, sems)

        P.op("sync", lambda E: E.dma_start(out=eye_sb[:], in_=eye_i), "dS", 16)
        P.op("sync", lambda E: E.dma_start(out=sel_sb[0:16, :], in_=sel_i), "dS", 16)
        P.op("sync", lambda E: E.dma_start(out=sel_sb[32:48, :], in_=sel_i), "dS", 16)
        Aap = A_sb[:].rearrange("p c n -> p (c n)")
        P.op("gpsimd", lambda E: E.dma_start(
            out=At_sb[:].rearrange("p (c n) -> p c n", n=NST),
            in_=A_log_i.rearrange("(c p) n -> p c n", p=128)), "dG", 16)
        P.op("gpsimd", lambda E: E.dma_start(
            out=bdt_sb[:].rearrange("p (c o) -> p c o", o=1), in_=b_dt_i.rearrange("(c p) o -> p c o", p=128), allow_slow_non_contiguous=True), "dG", 16)
        P.op("gpsimd", lambda E: E.dma_start(
            out=cbv_sb[:].rearrange("p (c o) -> p c o", o=1), in_=conv_b_i.rearrange("(c p) o -> p c o", p=128), allow_slow_non_contiguous=True), "dG", 16)
        P.op("gpsimd", lambda E: E.dma_start(
            out=Dv_sb[:].rearrange("p (c o) -> p c o", o=1), in_=D_i.rearrange("(c p) o -> p c o", p=128), allow_slow_non_contiguous=True), "dG", 16)
        P.op("gpsimd", lambda E: E.dma_start(
            out=cw_sb[:],
            in_=conv_w_i.rearrange("(c p) k -> p c k", p=128)), "dG", 16)
        P.op("gpsimd", lambda E: E.dma_start(
            out=b1_sb[:].rearrange("p (c o) -> p c o", o=1), in_=b1_i.rearrange("(c p) o -> p c o", p=128), allow_slow_non_contiguous=True), "dG", 16)
        P.op("gpsimd", lambda E: E.dma_start(
            out=b2_sb[:].rearrange("p (c o) -> p c o", o=1), in_=b2_i.rearrange("(c p) o -> p c o", p=128), allow_slow_non_contiguous=True), "dG", 16)

        for i, v in enumerate(consts):
            P.op("vector", lambda E, i=i, v=v: E.memset(cst_sb[:, i : i + 1], v))
        P.op("vector", lambda E: E.memset(onesf_sb[:, :], 1.0))
        P.op("vector", lambda E: E.memset(invD_sb[:, :], 1.0 / 1024.0))
        P.op("vector", lambda E: E.memset(carry_sb[:, :], 0.0))
        P.op("vector", lambda E: E.memset(pmax_sb[:, :], -1e30))
        P.op("vector", lambda E: E.memset(hmax_sb[:, :], -1e30))
        P.wait_now("vector", "dS")
        cst_done = P.op("vector", lambda E: E.tensor_copy(eyef_sb[:], eye_sb[:]), "vV")

        P.wait_now("scalar", "dG")
        e1 = P.op("scalar", lambda E: E.activation(Aap, At_sb[:], AF.Exp, scale=1.0), "sS")
        P.wait("scalar", "sS", e1)
        e2 = P.op("scalar", lambda E: E.mul(At_sb[:], Aap, -1.0), "sS")
        P.wait("scalar", "sS", e2)
        P.op("scalar", lambda E: E.copy(Aap, At_sb[:]), "sS")

        P.barrier_all()

        xTv = xT.rearrange("(c p) t -> p c t", p=128)

        if active("M1"):
            with contextlib.ExitStack() as ctx:
                ee = ctx.enter_context
                Wi_sb = ee(nc.sbuf_tensor("Wi_sb", [128, ECD, 2 * DI], BF16))
                Wx_sb = ee(nc.sbuf_tensor("Wx_sb", [128, DC, 128], BF16))
                diag_sb = ee(nc.sbuf_tensor("diag_sb", [128, DC * 4, 128], BF16))
                xk_sb = ee(nc.sbuf_tensor("xk_sb", [128, ECD, L1], BF16))
                xip_sb = ee(nc.sbuf_tensor("xip_sb", [128, DC, L1 + 3], BF16))
                xcv_sb = ee(nc.sbuf_tensor("xcv_sb", [128, DC, L1], BF16))
                szt_sb = ee(nc.sbuf_tensor("szt_sb", [128, DC, L1], BF16))
                psA = [psG[0], psG[1]]
                psC = [psG[2], psG[3]]
                psX = psG[4]

                wload = P.op("gpsimd", lambda E: E.dma_start(
                    out=Wi_sb[:],
                    in_=W_inT.rearrange("(c p) m -> p c m", p=128)), "dG", 16)
                wload = P.op("gpsimd", lambda E: E.dma_start(
                    out=Wx_sb[:],
                    in_=W_xTp.rearrange("(c p) m -> p c m", p=128)), "dG", 16)

                for dc in range(DC):
                    for kk in range(4):
                        last_diag = P.op("vector", lambda E, dc=dc, kk=kk: E.tensor_scalar(
                            diag_sb[:, dc * 4 + kk, :], eye_sb[:],
                            cw_sb[:, dc, kk : kk + 1], None, ALU.mult), "vV")
                for dc in range(DC):
                    P.op("vector", lambda E, dc=dc: E.memset(xip_sb[:, dc, 0:3], 0.0))
                pad_done = P.op("vector", lambda E: E.memset(tiny_sb[:, 0:1], 0.0), "vV")

                evac_done = {}
                conv_evac = {}
                store_done = {}
                xz_mm = {}
                conv_mm = {}
                xload = {}

                for k in range(NT1):
                    ks = slice(k * L1, (k + 1) * L1)
                    P.barrier_all()
                    xload[k] = P.op("gpsimd", lambda E, ks=ks: E.dma_start(
                        out=xk_sb[:],
                        in_=xTv[:, :, ks]), "dG", 16)

                    P.wait("tensor", "dG", xload[k])
                    if k >= 1:
                        P.wait("scalar", "dS", store_done[k - 1])
                    for ec in range(2 * DC):
                        prev = (k, ec - 2) if ec >= 2 else ((k - 1, 2 * DC - 2 + ec) if k >= 1 else None)
                        if prev is not None:
                            P.wait("tensor", "sS", evac_done[prev])
                        for dk in range(ECD):
                            xz_mm[(k, ec)] = P.op(
                                "tensor",
                                lambda E, ec=ec, dk=dk: E.matmul(
                                    psA[ec % 2][:],
                                    Wi_sb[:, dk, ec * 128 : (ec + 1) * 128],
                                    xk_sb[:, dk, :],
                                    start=(dk == 0), stop=(dk == ECD - 1)),
                                "tT" if dk == ECD - 1 else None)
                        P.wait("scalar", "tT", xz_mm[(k, ec)])
                        if ec < DC:
                            dc = ec
                            if k >= 1:
                                P.wait("scalar", "tT", conv_mm[(k - 1, dc)])
                                cce = P.op("scalar", lambda E, dc=dc: E.copy(
                                    xip_sb[:, dc, 0:3], xip_sb[:, dc, L1 : L1 + 3]), "sS")
                                P.wait("scalar", "sS", cce)
                            evac_done[(k, ec)] = P.op(
                                "scalar", lambda E, dc=dc, ec=ec: E.copy(
                                    xip_sb[:, dc, 3 : L1 + 3], psA[ec % 2][:]), "sS")
                        else:
                            evac_done[(k, ec)] = P.op(
                                "scalar", lambda E, ec=ec: E.activation(
                                    szt_sb[:, ec - DC, :], psA[ec % 2][:], AF.Silu), "sS")

                    P.wait("tensor", "vV", pad_done)
                    for dc in range(DC):
                        P.wait("tensor", "sS", evac_done[(k, dc)])
                        prev = (k, dc - 2) if dc >= 2 else ((k - 1, DC - 2 + dc) if k >= 1 else None)
                        if prev is not None:
                            P.wait("tensor", "sS", conv_evac[prev])
                        for kk in range(4):
                            conv_mm[(k, dc)] = P.op(
                                "tensor", lambda E, dc=dc, kk=kk: E.matmul(
                                    psC[dc % 2][:], diag_sb[:, dc * 4 + kk, :],
                                    xip_sb[:, dc, kk : kk + L1],
                                    start=(kk == 0), stop=(kk == 3)),
                                "tT" if kk == 3 else None)
                        P.wait("scalar", "tT", conv_mm[(k, dc)])
                        conv_evac[(k, dc)] = P.op(
                            "scalar", lambda E, dc=dc: E.activation(
                                xcv_sb[:, dc, :], psC[dc % 2][:], AF.Silu,
                                bias=cbv_sb[:, dc : dc + 1]), "sS")

                    P.wait("tensor", "sS", conv_evac[(k, DC - 1)])
                    for kdc in range(DC):
                        xdbl_mm = P.op("tensor", lambda E, kdc=kdc: E.matmul(
                            psX[:], Wx_sb[:, kdc, :], xcv_sb[:, kdc, :],
                            start=(kdc == 0), stop=(kdc == DC - 1)),
                            "tT" if kdc == DC - 1 else None)
                    P.wait("scalar", "tT", xdbl_mm)
                    xdbl_evac = P.op("scalar", lambda E, ks=ks: E.copy(
                        xdbl_sb[:, ks], psX[:]), "sS")

                    P.wait("sync", "sS", conv_evac[(k, DC - 1)])
                    P.op("sync", lambda E, ks=ks: E.dma_start(
                        out=xconv_sp[:, :, ks],
                        in_=xcv_sb[:]), "dS", 16)
                    P.wait("sync", "sS", evac_done[(k, 2 * DC - 1)])
                    store_done[k] = P.op("sync", lambda E, ks=ks: E.dma_start(
                        out=sz_sp[:, :, ks],
                        in_=szt_sb[:]), "dS", 16)

                P.barrier_all()

        if active("M2"):
            with contextlib.ExitStack() as ctx:
                ee = ctx.enter_context
                Wdt_sb = ee(nc.sbuf_tensor("Wdt_sb", [128, DC, 128], BF16))
                xcv2_sb = ee(nc.sbuf_tensor("xcv2_sb", [128, DC, L1], BF16))
                szt2_sb = ee(nc.sbuf_tensor("szt2_sb", [128, DC, L1], BF16))
                dt_sb = ee(nc.sbuf_tensor("dt_sb", [128, DC, L1], F32))
                dte_sb = ee(nc.sbuf_tensor("dte_sb", [128, L1], F32))
                u_sb = ee(nc.sbuf_tensor("u_sb", [128, DC, L1], BF16))
                Bb_sb = ee(nc.sbuf_tensor("Bb_sb", [128, NST, L1], BF16))
                Cb_sb = ee(nc.sbuf_tensor("Cb_sb", [128, NST, L1], BF16))
                y2_sb = ee(nc.sbuf_tensor("y2_sb", [128, DC, L1], BF16))
                NB = 4
                dA_sb = [ee(nc.sbuf_tensor(f"dA{i}", [128, L1], F32)) for i in range(NB)]
                Zt_sb = [ee(nc.sbuf_tensor(f"Zt{i}", [128, L1], BF16)) for i in range(NB)]
                st_sb = [ee(nc.sbuf_tensor(f"st{i}", [128, L1], BF16)) for i in range(NB)]
                Pt_sb = [ee(nc.sbuf_tensor(f"Pt{i}", [128, L1], BF16)) for i in range(NB)]
                tg_sb = ee(nc.sbuf_tensor("tg_sb", [128, L1], F32))
                psD = [psG[0], psG[1]]
                psBC = [psG[2], psG[3]]
                psY = [psG[4], psG[5]]

                P.op("gpsimd", lambda E: E.dma_start(
                    out=Wdt_sb[64:128, :, :],
                    in_=W_dtT_i.rearrange("k (c m) -> k c m", m=128)), "dG", 16)

                dt_ev = {}
                bc_ev = {}
                u_done = {}
                dA_done = {}
                zi_done = {}
                scan_done = {}
                carry_done = {}
                scanP = {}
                ymm = {}
                gat = {}
                ld_done = {}
                st_done = {}

                for k in range(NT1):
                    ks = slice(k * L1, (k + 1) * L1)
                    if k == 0:
                        P.barrier_all()
                    else:
                        P.wait("sync", "vV", gat[(k - 1, DC - 1)])
                    P.op("sync", lambda E, ks=ks: E.dma_start(
                        out=xcv2_sb[:],
                        in_=xconv_sp[:, :, ks]), "dS", 16)
                    ld_done[k] = P.op("sync", lambda E, ks=ks: E.dma_start(
                        out=szt2_sb[:],
                        in_=sz_sp[:, :, ks]), "dS", 16)

                    P.wait_now("tensor", "dG")
                    if k >= 1:
                        pass
                    for dc in range(DC):
                        if dc >= 2:
                            P.wait("tensor", "sS", dt_ev[(k, dc - 2)])
                        elif k >= 1:
                            P.wait("tensor", "sS", dt_ev[(k - 1, DC - 2 + dc)])
                        dtmm = P.op("tensor", lambda E, dc=dc, ks=ks: E.matmul(
                            psD[dc % 2][:], Wdt_sb[64:128, dc, :], xdbl_sb[64:128, ks],
                            start=True, stop=True), "tT")
                        P.wait("scalar", "tT", dtmm)
                        if k >= 1:
                            P.wait("scalar", "pP" if zpool else "vV",
                                   u_done[(k - 1, dc)])
                        if (k, dc - 1) in dt_ev:
                            P.wait("scalar", "sS", dt_ev[(k, dc - 1)])
                        de = P.op("scalar", lambda E, dc=dc: E.activation(
                            dte_sb[:], psD[dc % 2][:], AF.Exp,
                            bias=bdt_sb[:, dc : dc + 1]), "sS")
                        P.wait("scalar", "sS", de)
                        dt_ev[(k, dc)] = P.op("scalar", lambda E, dc=dc: E.activation(
                            dt_sb[:, dc, :], dte_sb[:], AF.Ln, bias=1.0), "sS")

                    if k >= 1:
                        P.wait("scalar", "vV", gat[(k - 1, DC - 1)])
                    for n in range(NST):
                        for which, src_lo, selbase, dst in (
                                (0, 0, 0, Bb_sb), (1, 32, 32, Cb_sb)):
                            idx = n * 2 + which
                            if idx >= 2:
                                P.wait("tensor", "sS", bc_ev[(k, idx - 2)])
                            elif k >= 1:
                                P.wait("tensor", "sS", bc_ev[(k - 1, 2 * NST - 2 + idx)])
                            bcm = P.op("tensor", lambda E, n=n, src_lo=src_lo,
                                       selbase=selbase, ks=ks, idx=idx: E.matmul(
                                           psBC[idx % 2][:],
                                           sel_sb[selbase : selbase + 16,
                                                  n * 128 : (n + 1) * 128],
                                           xdbl_sb[src_lo : src_lo + 16, ks],
                                           start=True, stop=True), "tT")
                            P.wait("scalar", "tT", bcm)
                            bc_ev[(k, idx)] = P.op(
                                "scalar", lambda E, n=n, dst=dst, idx=idx: E.copy(
                                    dst[:, n, :], psBC[idx % 2][:]), "sS")

                    ZENG = "gpsimd" if zpool else "vector"
                    ZSEM = "pP" if zpool else "vV"
                    P.wait(ZENG, "dS", ld_done[k])
                    for dc in range(DC):
                        P.wait(ZENG, "sS", dt_ev[(k, dc)])
                        u_done[(k, dc)] = P.op(ZENG, lambda E, dc=dc: E.tensor_tensor(
                            u_sb[:, dc, :], dt_sb[:, dc, :], xcv2_sb[:, dc, :],
                            ALU.mult), ZSEM)

                    if k >= 1:
                        P.wait("vector", "dS", st_done[k - 1])
                    step = 0
                    for dc in range(DC):
                        for n in range(NST):
                            i2 = step % 2
                            iz = step % NB
                            cidx = dc * NST + n
                            if step >= 2:
                                P.wait("scalar", "vV", scanP[(k, step - 2)])
                            elif k >= 1:
                                P.wait("scalar", "vV", scanP[(k - 1, 254 + step)])
                            dA_done[(k, step)] = P.op(
                                "scalar", lambda E, dc=dc, n=n, i2=i2: E.activation(
                                    dA_sb[i2][:], dt_sb[:, dc, :], AF.Exp,
                                    scale=A_sb[:, dc, n : n + 1]), "sS")
                            P.wait(ZENG, "sS", bc_ev[(k, 2 * n)])
                            prevz = (k, step - NB) if step >= NB else (
                                (k - 1, 256 - NB + step) if k >= 1 else None)
                            if prevz is not None:
                                P.wait(ZENG, "vV", scan_done[prevz])
                            zi_done[(k, step)] = P.op(
                                ZENG, lambda E, dc=dc, n=n, iz=iz:
                                E.tensor_tensor(
                                    Zt_sb[iz][:], u_sb[:, dc, :],
                                    Bb_sb[:, n, :], ALU.mult), ZSEM)
                            P.wait("vector", "sS", dA_done[(k, step)])
                            if zpool:
                                P.wait("vector", "pP", zi_done[(k, step)])
                            if k == 0:
                                si = P.op("vector", lambda E, i2=i2, iz=iz:
                                          E.tensor_tensor_scan(
                                              st_sb[i2][:], dA_sb[i2][:],
                                              Zt_sb[iz][:],
                                              0.0, ALU.mult, ALU.add), "vV")
                            else:
                                si = P.op("vector", lambda E, i2=i2, iz=iz,
                                          cidx=cidx:
                                          E.tensor_tensor_scan(
                                              st_sb[i2][:], dA_sb[i2][:],
                                              Zt_sb[iz][:],
                                              carry_sb[:, cidx : cidx + 1],
                                              ALU.mult, ALU.add), "vV")
                            scan_done[(k, step)] = si
                            P.wait("vector", "vV", si)
                            P.op("vector", lambda E, i2=i2, cidx=cidx: E.tensor_copy(
                                carry_sb[:, cidx : cidx + 1],
                                st_sb[i2][:, L1 - 1 : L1]), "vV")
                            if step >= 2:
                                P.wait("vector", "tT", ymm[(k, step - 2)])
                            elif k >= 1:
                                P.wait("vector", "tT", ymm[(k - 1, 254 + step)])
                            P.wait("vector", "sS", bc_ev[(k, 2 * n + 1)])
                            scanP[(k, step)] = P.op(
                                "vector", lambda E, i2=i2, n=n: E.tensor_tensor(
                                    Pt_sb[i2][:], st_sb[i2][:], Cb_sb[:, n, :],
                                    ALU.mult), "vV")
                            P.wait("tensor", "vV", scanP[(k, step)])
                            if n == 0 and dc >= 2:
                                P.wait("tensor", "vV", gat[(k, dc - 2)])
                            elif n == 0 and k >= 1 and dc < 2:
                                P.wait("tensor", "vV", gat[(k - 1, DC - 2 + dc)])
                            ymm[(k, step)] = P.op(
                                "tensor", lambda E, i2=i2, dc=dc, n=n:
                                E.matmul(psY[dc % 2][:], eye_sb[:],
                                         Pt_sb[i2][:],
                                         start=(n == 0), stop=(n == NST - 1)),
                                "tT")
                            step += 1
                        P.wait("vector", "tT", ymm[(k, step - 1)])
                        tgi = P.op("vector", lambda E, dc=dc: E.tensor_tensor(
                            tg_sb[:], psY[dc % 2][:], szt2_sb[:, dc, :], ALU.mult), "vV")
                        P.wait("vector", "vV", tgi)
                        gat[(k, dc)] = P.op(
                            "vector", lambda E, dc=dc: E.scalar_tensor_tensor(
                                y2_sb[:, dc, :], xcv2_sb[:, dc, :],
                                Dv_sb[:, dc : dc + 1], tg_sb[:],
                                ALU.mult, ALU.add), "vV")

                    P.wait("sync", "vV", gat[(k, DC - 1)])
                    st_done[k] = P.op("sync", lambda E, ks=ks: E.dma_start(
                        out=y2_sp[:, :, ks],
                        in_=y2_sb[:]), "dS", 16)
                    if debug_spills:
                        P.wait("sync", "sS", dt_ev[(k, DC - 1)])
                        dbg_done = P.op("sync", lambda E, ks=ks: E.dma_start(
                            out=dbg_sp[:, :, ks],
                            in_=dt_sb[:]), "dS", 16)
                        st_done[k] = dbg_done
                        P.wait("scalar", "dS", dbg_done)

                P.barrier_all()

        def layer_norm_tiles(Ltile, v_sb, vsq_sb, x1out_sb, psm, psq, psB, necd=ECD):
            mean_s = stat_sb[:, 0:Ltile]
            msq_s = stat_sb[:, 512 : 512 + Ltile]
            var_s = stat_sb[:, 1024 : 1024 + Ltile]
            rstd_s = stat_sb[:, 1536 : 1536 + Ltile]
            mrstd_s = stat_sb[:, 2048 : 2048 + Ltile]
            tmp_s = stat_sb[:, 2560 : 2560 + Ltile]
            tmp2_s = stat_sb[:, 2560 + 512 : 2560 + 512 + Ltile]
            sq = P.op("scalar", lambda E: E.activation(
                vsq_sb[:].rearrange("p c l -> p (c l)"),
                v_sb[:].rearrange("p c l -> p (c l)"), AF.Square), "sS")
            P.wait("tensor", "sS", sq)
            for ec in range(necd):
                P.op("tensor", lambda E, ec=ec: E.matmul(
                    psm[0:1, :Ltile], invD_sb[:], v_sb[:, ec, :],
                    start=(ec == 0), stop=(ec == necd - 1)))
            for ec in range(necd):
                smm = P.op("tensor", lambda E, ec=ec: E.matmul(
                    psq[0:1, :Ltile], invD_sb[:], vsq_sb[:, ec, :],
                    start=(ec == 0), stop=(ec == necd - 1)),
                    "tT" if ec == necd - 1 else None)
            P.wait("scalar", "tT", smm)
            P.op("scalar", lambda E: E.copy(mean_s, psm[0:1, :Ltile]))
            ecop = P.op("scalar", lambda E: E.copy(msq_s, psq[0:1, :Ltile]), "sS")
            P.wait("vector", "sS", ecop)
            tmpi = P.op("vector", lambda E: E.tensor_tensor(
                tmp_s, mean_s, mean_s, ALU.mult), "vV")
            P.wait("vector", "vV", tmpi)
            vv = P.op("vector", lambda E: E.tensor_tensor(
                var_s, msq_s, tmp_s, ALU.subtract), "vV")
            P.wait("scalar", "vV", vv)
            lni = P.op("scalar", lambda E: E.activation(
                tmp2_s, var_s, AF.Ln, bias=EPSLN), "sS")
            P.wait("scalar", "sS", lni)
            rr = P.op("scalar", lambda E: E.activation(
                rstd_s, tmp2_s, AF.Exp, scale=-0.5), "sS")
            P.wait("vector", "sS", rr)
            mm = P.op("vector", lambda E: E.tensor_tensor(
                mrstd_s, mean_s, rstd_s, ALU.mult), "vV")
            P.wait("tensor", "vV", mm)
            P.op("tensor", lambda E: E.matmul(
                psB[0][0:128, :Ltile], onesf_sb[:], rstd_s, start=True, stop=True))
            bb = P.op("tensor", lambda E: E.matmul(
                psB[1][0:128, :Ltile], onesf_sb[:], mrstd_s, start=True, stop=True), "tT")
            P.wait("vector", "tT", bb)
            for ec in range(necd):
                mi = P.op("vector", lambda E, ec=ec: E.tensor_tensor(
                    vsq_sb[:, ec, :], v_sb[:, ec, :], psB[0][0:128, :Ltile], ALU.mult), "vV")
                P.wait("vector", "vV", mi)
                lnv = P.op("vector", lambda E, ec=ec: E.tensor_tensor(
                    x1out_sb[:, ec, :], vsq_sb[:, ec, :], psB[1][0:128, :Ltile],
                    ALU.subtract), "vV")
            return lnv

        if active("M3"):
            with contextlib.ExitStack() as ctx:
                ee = ctx.enter_context
                Wo_sb = ee(nc.sbuf_tensor("Wo_sb", [128, DC, D], BF16))
                y2l_sb = ee(nc.sbuf_tensor("y2l_sb", [128, DC, L1], BF16))
                xf_sb = ee(nc.sbuf_tensor("xf_sb", [128, ECD, L1], F32))
                v_sb = ee(nc.sbuf_tensor("v_sb", [128, ECD, L1], F32))
                vsq_sb = ee(nc.sbuf_tensor("vsq_sb", [128, ECD, L1], F32))
                x1_sb = ee(nc.sbuf_tensor("x1_sb", [128, ECD, L1], F32))
                psO = [psG[0], psG[1]]
                psm = psG[2]
                psq = psG[3]
                psB = [psG[4], psG[5]]

                P.op("gpsimd", lambda E: E.dma_start(
                    out=Wo_sb[:],
                    in_=W_outT_i.rearrange("(c p) m -> p c m", p=128)), "dG", 16)

                stx = None
                for k in range(NT1):
                    ks = slice(k * L1, (k + 1) * L1)
                    P.barrier_all()
                    P.op("sync", lambda E, ks=ks: E.dma_start(
                        out=y2l_sb[:],
                        in_=y2_sp[:, :, ks]), "dS", 16)
                    ldk = P.op("sync", lambda E, ks=ks: E.dma_start(
                        out=xf_sb[:],
                        in_=xTv[:, :, ks]), "dS", 16)
                    P.wait("tensor", "dS", ldk)
                    P.wait_now("tensor", "dG")
                    vadds = {}
                    for ec in range(ECD):
                        if ec >= 2:
                            P.wait("tensor", "vV", vadds[ec - 2])
                        for kdc in range(DC):
                            omm = P.op("tensor", lambda E, ec=ec, kdc=kdc: E.matmul(
                                psO[ec % 2][:], Wo_sb[:, kdc, ec * 128 : (ec + 1) * 128],
                                y2l_sb[:, kdc, :],
                                start=(kdc == 0), stop=(kdc == DC - 1)),
                                "tT" if kdc == DC - 1 else None)
                        P.wait("vector", "tT", omm)
                        vadds[ec] = P.op("vector", lambda E, ec=ec: E.tensor_tensor(
                            v_sb[:, ec, :], psO[ec % 2][:], xf_sb[:, ec, :],
                            ALU.add), "vV")
                    P.wait("scalar", "vV", vadds[ECD - 1])
                    P.wait("tensor", "vV", vadds[ECD - 1])
                    lnv = layer_norm_tiles(L1, v_sb, vsq_sb, x1_sb, psm, psq, psB)
                    P.wait("vector", "vV", lnv)
                    ri = P.op("vector", lambda E: E.tensor_reduce(
                        tiny_sb[:, 0:1], x1_sb[:].rearrange("p c l -> p (c l)"),
                        AX.X, ALU.max), "vV")
                    P.wait("vector", "vV", ri)
                    a, b = k % 2, (k + 1) % 2
                    pm = P.op("vector", lambda E, a=a, b=b: E.tensor_tensor(
                        pmax_sb[:, b : b + 1], pmax_sb[:, a : a + 1],
                        tiny_sb[:, 0:1], ALU.max), "vV")
                    P.wait("sync", "vV", lnv)
                    stx = P.op("sync", lambda E, ks=ks: E.dma_start(
                        out=x1_sp[:, :, ks],
                        in_=x1_sb[:]), "dS", 16)
                    P.wait("sync", "vV", pm)
                    P.wait("sync", "dS", stx)

                P.barrier_all()

        def allreduce_max(maxv_sb, sinv_sb, rsv_sb, ccin, ccout, psB0):
            tr = P.op("tensor", lambda E: E.transpose(
                psB0[0:1, 0:128], maxv_sb[:], eyef_sb[:]), "tT")
            P.wait("vector", "tT", tr)
            e1 = P.op("vector", lambda E: E.reduce_max(
                tiny_sb[0:1, 0:1], psB0[0:1, 0:128], AX.X), "vV")
            P.wait("vector", "vV", e1)
            gm = P.op("vector", lambda E: E.tensor_scalar(
                tiny_sb[0:1, 1:2], tiny_sb[0:1, 0:1], EPSQ, 1.0 / 255.0,
                ALU.max, ALU.mult), "vV")
            P.wait("tensor", "vV", gm)
            bc = P.op("tensor", lambda E: E.matmul(
                psB0[0:128, 0:1], onesf_sb[:], tiny_sb[0:1, 1:2],
                start=True, stop=True), "tT")
            P.wait("scalar", "tT", bc)
            sc = P.op("scalar", lambda E: E.copy(sinv_sb[:], psB0[0:128, 0:1]), "sS")
            P.wait("vector", "sS", sc)
            b16, _ = bass.broadcast_tensor_aps(sinv_sb[:, 0:1], tiny_sb[:, 0:16])
            vb = P.op("vector", lambda E, b16=b16: E.tensor_copy(
                tiny_sb[:, 0:16], b16), "vV")
            P.wait("gpsimd", "vV", vb)
            dg1 = P.op("gpsimd", lambda E: E.dma_start(
                out=ccin, in_=tiny_sb[:, 0:16]), "dG", 16)
            P.wait("gpsimd", "dG", dg1)
            cc = P.op("gpsimd", lambda E: E.collective_compute(
                "AllReduce", ALU.max,
                replica_groups=[[0, 1, 2, 3, 4, 5, 6, 7]],
                ins=[ccin], outs=[ccout]), "CC")
            P.wait("gpsimd", "CC", cc)
            dg2 = P.op("gpsimd", lambda E: E.dma_start(
                out=tiny_sb[:, 0:16], in_=ccout), "dG", 16)
            P.wait("vector", "dG", dg2)
            e2 = P.op("vector", lambda E: E.tensor_copy(sinv_sb[:], tiny_sb[:, 0:1]), "vV")
            P.wait("vector", "vV", e2)
            rv = P.op("vector", lambda E: E.reciprocal(rsv_sb[:], sinv_sb[:]), "vV")
            return rv

        if active("AR1"):
            allreduce_max(pmax_sb[:, 0:1], sin1_sb, rs1_sb, cc1_in, cc1_out, psAR_g)
            P.barrier_all()

        def wq_phase(win, nchunk, width, sw_sb, wi_sb, nd_out, b_sb, se_sb, bq_sb,
                     sin_sb, name):
            with contextlib.ExitStack() as ctx:
                ee = ctx.enter_context
                wch = ee(nc.sbuf_tensor(f"wch_{name}", [128, width], F32))
                q32 = ee(nc.sbuf_tensor(f"q32_{name}", [128, width], I32))
                qbf = ee(nc.sbuf_tensor(f"qbf_{name}", [128, width], BF16))
                rw = ee(nc.sbuf_tensor(f"rw_{name}", [128, 1], F32))
                tq = ee(nc.sbuf_tensor(f"tq_{name}", [128, width], F32))
                q32b = ee(nc.sbuf_tensor(f"q32b_{name}", [128, nchunk], I32))
                tmpb = ee(nc.sbuf_tensor(f"tmpb_{name}", [128, nchunk], F32))
                tmpb2 = ee(nc.sbuf_tensor(f"tmpb2_{name}", [128, nchunk], F32))
                psT = psT_g
                winv = win.rearrange("(c p) d -> p c d", p=128)
                ndc = width // 128
                for fcx in range(nchunk):
                    P.barrier_all()
                    ld = P.op("sync", lambda E, fcx=fcx: E.dma_start(
                        out=wch[:], in_=winv[:, fcx, :]), "dS", 16)
                    P.wait("vector", "dS", ld)
                    if fcx >= 1:
                        P.wait("vector", "sS", trev)
                    e1 = P.op("vector", lambda E, fcx=fcx: E.tensor_reduce(
                        tiny_sb[:, 4:5], wch[:], AX.X, ALU.max,
                        apply_absolute_value=True), "vV")
                    P.wait("vector", "vV", e1)
                    e2 = P.op("vector", lambda E, fcx=fcx: E.tensor_scalar(
                        sw_sb[:, fcx : fcx + 1], tiny_sb[:, 4:5],
                        EPSQ, 1.0 / 127.0, ALU.max, ALU.mult), "vV")
                    P.wait("vector", "vV", e2)
                    e3 = P.op("vector", lambda E, fcx=fcx: E.reciprocal(
                        rw[:], sw_sb[:, fcx : fcx + 1]), "vV")
                    P.wait("vector", "vV", e3)
                    e4 = P.op("vector", lambda E: E.tensor_scalar(
                        tq[:], wch[:], rw[:, 0:1], None, ALU.mult), "vV")
                    P.wait("vector", "vV", e4)
                    e4c = P.op("vector", lambda E: E.tensor_copy(
                        q32[:], tq[:]), "vV")
                    P.wait("vector", "vV", e4c)
                    qdone = P.op("vector", lambda E: E.tensor_scalar(
                        qbf[:], q32[:], -128.0, 127.0, ALU.max, ALU.min), "vV")
                    P.wait("tensor", "vV", qdone)
                    for dcc in range(ndc):
                        if dcc >= 1:
                            P.wait("tensor", "sS", trev)
                        trm = P.op("tensor", lambda E, dcc=dcc: E.transpose(
                            psT[:], qbf[:, dcc * 128 : (dcc + 1) * 128],
                            eye_sb[:]), "tT")
                        P.wait("scalar", "tT", trm)
                        trev = P.op("scalar", lambda E, dcc=dcc, fcx=fcx: E.copy(
                            wi_sb[:, dcc, fcx * 128 : (fcx + 1) * 128], psT[:]), "sS")
                e1 = P.op("vector", lambda E: E.tensor_scalar(
                    se_sb[:], sw_sb[:], sin_sb[:, 0:1], None, ALU.mult), "vV")
                P.wait("vector", "vV", e1)
                e2 = P.op("vector", lambda E: E.reciprocal(tmpb[:], se_sb[:]), "vV")
                P.wait("vector", "vV", e2)
                e3 = P.op("vector", lambda E: E.tensor_tensor(
                    tmpb2[:], b_sb[:], tmpb[:], ALU.mult), "vV")
                P.wait("vector", "vV", e3)
                e4 = P.op("vector", lambda E: E.tensor_copy(
                    q32b[:], tmpb2[:]), "vV")
                P.wait("vector", "vV", e4)
                bqd = P.op("vector", lambda E: E.tensor_tensor(
                    bq_sb[:], q32b[:], se_sb[:], ALU.mult), "vV")
                return bqd

        if active("WQ1"):
            wq1_ctx = top.enter_context(contextlib.ExitStack())
            w1i_sb = wq1_ctx.enter_context(
                nc.sbuf_tensor("w1i_sb", [128, ECD, FF], BF16))
            wq_phase(w1n_i, FC, D, sw1_sb, w1i_sb, ECD, b1_sb, se1_sb, bq1_sb,
                     sin1_sb, "w1")
            P.barrier_all()
            if debug_spills:
                P.op("gpsimd", lambda E: E.dma_start(out=w1i_dbg, in_=w1i_sb[:]), "dG", 16)
                P.op("gpsimd", lambda E: E.dma_start(out=q1_dbg[:, 0:FC], in_=sw1_sb[:]), "dG", 16)
                P.op("gpsimd", lambda E: E.dma_start(out=q1_dbg[:, FC:2 * FC], in_=se1_sb[:]), "dG", 16)
                P.op("gpsimd", lambda E: E.dma_start(out=q1_dbg[:, 2 * FC:3 * FC], in_=bq1_sb[:]), "dG", 16)
                P.op("gpsimd", lambda E: E.dma_start(out=q1_dbg[:, 3 * FC:3 * FC + 1], in_=sin1_sb[:], allow_slow_non_contiguous=True), "dG", 16)
                P.wait_now("gpsimd", "dG")
                P.barrier_all()

        if active("F1"):
            with contextlib.ExitStack() as ctx:
                ee = ctx.enter_context
                x1l_sb = ee(nc.sbuf_tensor("x1l_sb", [128, ECD, L1], F32))
                xi32_sb = ee(nc.sbuf_tensor("xi32_sb", [128, L1], I32))
                xi_sb = ee(nc.sbuf_tensor("xi_sb", [128, ECD, L1], BF16))
                h_sb = ee(nc.sbuf_tensor("h_sb", [128, FC, L1], F32))
                psH = [psG[0], psG[1]]

                hev = {}
                for k in range(NT1):
                    ks = slice(k * L1, (k + 1) * L1)
                    P.barrier_all()
                    ld = P.op("sync", lambda E, ks=ks: E.dma_start(
                        out=x1l_sb[:],
                        in_=x1_sp[:, :, ks]), "dS", 16)
                    P.wait("vector", "dS", ld)
                    for ec in range(ECD):
                        if k >= 1 and ec == 0:
                            P.wait("vector", "tT", hmm_last)
                        if ec >= 1:
                            P.wait("vector", "vV", qd)
                        q32i = P.op("vector", lambda E, ec=ec: E.tensor_scalar(
                            xi32_sb[:], x1l_sb[:, ec, :], rs1_sb[:, 0:1], None,
                            ALU.mult), "vV")
                        P.wait("vector", "vV", q32i)
                        qd = P.op("vector", lambda E, ec=ec: E.tensor_scalar(
                            xi_sb[:, ec, :], xi32_sb[:], 0.0, 255.0,
                            ALU.max, ALU.min), "vV")
                    P.wait("tensor", "vV", qd)
                    if debug_spills:
                        P.wait("gpsimd", "vV", qd)
                        P.op("gpsimd", lambda E, ks=ks: E.dma_start(
                            out=xi_dbg[:, :, ks], in_=xi_sb[:]), "dG", 16)
                    for fcx in range(FC):
                        prev = fcx - 2 if fcx >= 2 else (FC - 2 + fcx if k >= 1 else None)
                        if prev is not None and (k if fcx >= 2 else k - 1, prev) in hev:
                            P.wait("tensor", "sS",
                                   hev[(k if fcx >= 2 else k - 1, prev)])
                        for kd in range(ECD):
                            hmm_last = P.op("tensor", lambda E, fcx=fcx, kd=kd: E.matmul(
                                psH[fcx % 2][:],
                                w1i_sb[:, kd, fcx * 128 : (fcx + 1) * 128],
                                xi_sb[:, kd, :],
                                start=(kd == 0), stop=(kd == ECD - 1)),
                                "tT" if kd == ECD - 1 else None)
                        P.wait("scalar", "tT", hmm_last)
                        if k >= 1 and fcx == 0:
                            P.wait("scalar", "dS", hst)
                            P.wait("scalar", "vV", hm)
                        hev[(k, fcx)] = P.op("scalar", lambda E, fcx=fcx: E.activation(
                            h_sb[:, fcx, :], psH[fcx % 2][:], AF.Relu,
                            scale=se1_sb[:, fcx : fcx + 1],
                            bias=bq1_sb[:, fcx : fcx + 1]), "sS")
                    P.wait("vector", "sS", hev[(k, FC - 1)])
                    ri = P.op("vector", lambda E: E.tensor_reduce(
                        tiny_sb[:, 2:3], h_sb[:].rearrange("p c l -> p (c l)"),
                        AX.X, ALU.max), "vV")
                    P.wait("vector", "vV", ri)
                    a, b = k % 2, (k + 1) % 2
                    hm = P.op("vector", lambda E, a=a, b=b: E.tensor_tensor(
                        hmax_sb[:, b : b + 1], hmax_sb[:, a : a + 1],
                        tiny_sb[:, 2:3], ALU.max), "vV")
                    P.wait("sync", "sS", hev[(k, FC - 1)])
                    hst = P.op("sync", lambda E, ks=ks: E.dma_start(
                        out=h_sp[:, :, ks],
                        in_=h_sb[:]), "dS", 16)
                    P.wait("sync", "vV", hm)

                P.barrier_all()
            wq1_ctx.close()

        if active("AR2"):
            allreduce_max(hmax_sb[:, 0:1], sin2_sb, rs2_sb, cc2_in, cc2_out, psAR_g)
            P.barrier_all()

        if active("WQ2"):
            wq2_ctx = top.enter_context(contextlib.ExitStack())
            w2i_sb = wq2_ctx.enter_context(
                nc.sbuf_tensor("w2i_sb", [128, FC, D], BF16))
            wq_phase(w2n_i, ECD, FF, sw2_sb, w2i_sb, FC, b2_sb, se2_sb, bq2_sb,
                     sin2_sb, "w2")
            P.barrier_all()

        if active("F2"):
            with contextlib.ExitStack() as ctx:
                ee = ctx.enter_context
                h2_sb = ee(nc.sbuf_tensor("h2_sb", [128, FC, L2], F32))
                hi32_sb = ee(nc.sbuf_tensor("hi32_sb", [128, L2], I32))
                hi_sb = ee(nc.sbuf_tensor("hi_sb", [128, FC, L2], BF16))
                x12_sb = ee(nc.sbuf_tensor("x12_sb", [128, ECD, L2], F32))
                ffn_sb = ee(nc.sbuf_tensor("ffn_sb", [128, ECD, L2], F32))
                v2_sb = ee(nc.sbuf_tensor("v2_sb", [128, ECD, L2], F32))
                vsq2_sb = ee(nc.sbuf_tensor("vsq2_sb", [128, ECD, L2], F32))
                x2_sb = ee(nc.sbuf_tensor("x2_sb", [128, ECD, L2], F32))
                psO2 = [psG[0][:, 0:L2], psG[1][:, 0:L2]]
                psm2 = psG[2]
                psq2 = psG[3]
                psB2 = [psG[4], psG[5]]

                for k in range(NT2):
                    ks = slice(k * L2, (k + 1) * L2)
                    P.barrier_all()
                    P.op("sync", lambda E, ks=ks: E.dma_start(
                        out=h2_sb[:],
                        in_=h_sp[:, :, ks]), "dS", 16)
                    ld2 = P.op("sync", lambda E, ks=ks: E.dma_start(
                        out=x12_sb[:],
                        in_=x1_sp[:, :, ks]), "dS", 16)
                    P.wait("vector", "dS", ld2)
                    qd2_done = {}
                    for fcx in range(FC):
                        if k >= 1 and fcx == 0:
                            P.wait("vector", "tT", omm2)
                        if fcx >= 1:
                            P.wait("vector", "vV", qd2_done[fcx - 1])
                        q32i = P.op("vector", lambda E, fcx=fcx: E.tensor_scalar(
                            hi32_sb[:], h2_sb[:, fcx, :], rs2_sb[:, 0:1], None,
                            ALU.mult), "vV")
                        P.wait("vector", "vV", q32i)
                        qd2_done[fcx] = P.op("vector", lambda E, fcx=fcx: E.tensor_scalar(
                            hi_sb[:, fcx, :], hi32_sb[:], 0.0, 255.0,
                            ALU.max, ALU.min), "vV")
                    qd2 = qd2_done[FC - 1]
                    ev2 = {}
                    for ec in range(ECD):
                        if ec >= 2:
                            P.wait("tensor", "sS", ev2[ec - 2])
                        for kf in range(FC):
                            if ec == 0:
                                P.wait("tensor", "vV", qd2_done[kf])
                            omm2 = P.op("tensor", lambda E, ec=ec, kf=kf: E.matmul(
                                psO2[ec % 2],
                                w2i_sb[:, kf, ec * 128 : (ec + 1) * 128],
                                hi_sb[:, kf, :],
                                start=(kf == 0), stop=(kf == FC - 1)),
                                "tT" if kf == FC - 1 else None)
                        P.wait("scalar", "tT", omm2)
                        ev2[ec] = P.op("scalar", lambda E, ec=ec: E.activation(
                            ffn_sb[:, ec, :], psO2[ec % 2], AF.Identity,
                            scale=se2_sb[:, ec : ec + 1],
                            bias=bq2_sb[:, ec : ec + 1]), "sS")
                    for ec in range(ECD):
                        P.wait("vector", "sS", ev2[ec])
                        va2 = P.op("vector", lambda E, ec=ec: E.tensor_tensor(
                            v2_sb[:, ec, :], ffn_sb[:, ec, :], x12_sb[:, ec, :],
                            ALU.add), "vV")
                    P.wait("scalar", "vV", va2)
                    P.wait("tensor", "vV", va2)
                    lnv2 = layer_norm_tiles(L2, v2_sb, vsq2_sb, x2_sb, psm2, psq2, psB2)
                    P.wait("sync", "vV", lnv2)
                    stx2 = P.op("sync", lambda E, ks=ks: E.dma_start(
                        out=outT.rearrange("(c p) t -> p c t", p=128)[:, :, ks],
                        in_=x2_sb[:]), "dS", 16)

                P.barrier_all()

        if debug_spills:
            P.barrier_all()
            P.op("gpsimd", lambda E: E.dma_start(out=xconv_dbg, in_=xconv_sp), "dG", 16)
            P.op("gpsimd", lambda E: E.dma_start(out=sz_dbg, in_=sz_sp), "dG", 16)
            if active("M2"):
                P.op("gpsimd", lambda E: E.dma_start(out=y2_dbg, in_=y2_sp), "dG", 16)

        fin = P.snap()
        P.wait_snap("sync", fin)

        P.flush()

    return nc



def _prep_shared(inputs):
    f32 = np.float32
    W_in = np.asarray(inputs["L0_attn_W_in"], f32)
    W_x = np.asarray(inputs["L0_attn_W_x"], f32)
    W_dt = np.asarray(inputs["L0_attn_W_dt"], f32)
    W_out = np.asarray(inputs["L0_attn_W_out"], f32)
    conv_w = np.asarray(inputs["L0_attn_conv_w"], f32).reshape(DI, 4)
    Wxp = np.zeros((128, DI), f32)
    Wxp[0:16] = W_x[RK : RK + NST]
    Wxp[32:48] = W_x[RK + NST : RK + 2 * NST]
    Wxp[64:128] = W_x[0:RK]
    sel = np.zeros((16, 16 * 128), dtype=ml_dtypes.bfloat16)
    for n in range(16):
        sel[n, n * 128 : (n + 1) * 128] = 1
    shared = {
        "W_inT": np.ascontiguousarray(W_in.T),
        "W_xTp": np.ascontiguousarray(Wxp.T),
        "W_dtT": np.ascontiguousarray(W_dt.T),
        "W_outT": np.ascontiguousarray(W_out.T),
        "w1n": np.ascontiguousarray(np.asarray(inputs["L0_ffn_w1"], f32)),
        "w2n": np.ascontiguousarray(np.asarray(inputs["L0_ffn_w2"], f32)),
        "conv_w": np.ascontiguousarray(conv_w),
        "conv_b": np.asarray(inputs["L0_attn_conv_b"], f32).reshape(DI, 1),
        "b_dt": np.asarray(inputs["L0_attn_b_dt"], f32).reshape(DI, 1),
        "A_log": np.ascontiguousarray(np.asarray(inputs["L0_attn_A_log"], f32)),
        "D_vec": np.asarray(inputs["L0_attn_D"], f32).reshape(DI, 1),
        "b1": np.asarray(inputs["L0_ffn_b1"], f32).reshape(FF, 1),
        "b2": np.asarray(inputs["L0_ffn_b2"], f32).reshape(D, 1),
        "eye": np.eye(128, dtype=ml_dtypes.bfloat16),
        "sel": sel,
    }
    return shared


def kernel(**inputs):
    x = np.asarray(inputs["L0_attn_x"], np.float32)
    shared = _prep_shared(inputs)
    nc = build_program()
    in_maps = []
    for b in range(8):
        m = dict(shared)
        m["xT"] = np.ascontiguousarray(x[b].T)
        in_maps.append(m)
    res = run_bass_kernel_spmd(nc, in_maps, core_ids=list(range(8)))
    out = np.stack([np.ascontiguousarray(res.results[b]["outT"].T) for b in range(8)])
    return out.astype(np.float32)


# version 08 (final)
# speedup vs baseline: 1.0549x; 1.0148x over previous
import contextlib

import numpy as np
import ml_dtypes

import concourse.bass as bass
import concourse.mybir as mybir
from concourse.bass_utils import run_bass_kernel_spmd

F32 = mybir.dt.float32
BF16 = mybir.dt.bfloat16
I32 = mybir.dt.int32
AF = mybir.ActivationFunctionType
ALU = mybir.AluOpType
AX = mybir.AxisListType

T = 4096
D = 1024
DI = 2048
NST = 16
RK = 64
FF = 4096
ECD = D // 128
DC = DI // 128
FC = FF // 128
L1 = 512
NT1 = T // L1
L2 = 256
NT2 = T // L2

EPSQ = 1e-8
EPSLN = 1e-5

PHASES = ["P0", "M1", "M2", "M3", "AR1", "WQ1", "F1", "AR2", "WQ2", "F2"]
ENGINES = ["sync", "gpsimd", "vector", "scalar", "tensor"]


class Ctr:
    def __init__(self, handle):
        self.h = handle
        self.v = 0


class Prog:

    def __init__(self, nc, block, sems):
        self.nc = nc
        self.block = block
        self.sems = sems
        self.streams = {n: [] for n in ENGINES}

    def op(self, eng, fn, inc=None, n=1):
        if inc is not None:
            ctr = self.sems[inc]
            ctr.v += n
            nn = n

            def thunk(E, fn=fn, ctr=ctr, nn=nn):
                fn(E).then_inc(ctr.h, nn)
        else:
            def thunk(E, fn=fn):
                fn(E)
        self.streams[eng].append(thunk)
        return self.sems[inc].v if inc is not None else None

    def wait(self, eng, sem, val):
        if val <= 0:
            return
        h = self.sems[sem].h
        self.streams[eng].append(lambda E, h=h, val=val: E.wait_ge(h, val))

    def wait_now(self, eng, sem):
        self.wait(eng, sem, self.sems[sem].v)

    def snap(self):
        return {k: c.v for k, c in self.sems.items()}

    def wait_snap(self, eng, snapshot):
        for k, v in snapshot.items():
            self.wait(eng, k, v)

    def barrier_all(self):
        s = self.snap()
        for eng in ENGINES:
            self.wait_snap(eng, s)

    def flush(self):
        for name in ENGINES:
            st = self.streams[name]
            self.streams[name] = []
            if st:
                getattr(self.block, name)(lambda E, st=st: [f(E) for f in st])


def build_program(phase_limit="F2", debug_spills=False, zpool=False):
    nc = bass.Bass("TRN2", target_bir_lowering=False)
    nlim = PHASES.index(phase_limit)

    def active(ph):
        return PHASES.index(ph) <= nlim

    def din(name, shape, dtype=F32):
        return nc.dram_tensor(name, shape, dtype, kind="ExternalInput").ap()

    xT = din("xT", [D, T])
    W_inT = din("W_inT", [D, 2 * DI])
    W_xTp = din("W_xTp", [DI, 128])
    W_dtT_i = din("W_dtT", [RK, DI])
    W_outT_i = din("W_outT", [DI, D])
    w1n_i = din("w1n", [FF, D])
    w2n_i = din("w2n", [D, FF])
    conv_w_i = din("conv_w", [DI, 4])
    conv_b_i = din("conv_b", [DI, 1])
    b_dt_i = din("b_dt", [DI, 1])
    A_log_i = din("A_log", [DI, NST])
    D_i = din("D_vec", [DI, 1])
    b1_i = din("b1", [FF, 1])
    b2_i = din("b2", [D, 1])
    eye_i = din("eye", [128, 128], BF16)
    sel_i = din("sel", [16, 16 * 128], BF16)

    outT = nc.dram_tensor("outT", [D, T], F32, kind="ExternalOutput").ap()

    spk = "ExternalOutput" if debug_spills else "Internal"
    xconv_sp = nc.dram_tensor("xconv_sp", [128, DC, T], BF16, kind="Internal").ap()
    sz_sp = nc.dram_tensor("sz_sp", [128, DC, T], BF16, kind="Internal").ap()
    y2_sp = nc.dram_tensor("y2_sp", [128, DC, T], BF16, kind="Internal").ap()
    x1_sp = nc.dram_tensor("x1_sp", [128, ECD, T], F32, kind=spk).ap()
    h_sp = nc.dram_tensor("h_sp", [128, FC, T], F32, kind=spk).ap()
    dbg_sp = nc.dram_tensor("dbg_sp", [128, DC, T], F32, kind=spk).ap()
    if debug_spills:
        xconv_dbg = nc.dram_tensor("xconv_dbg", [128, DC, T], F32, kind=spk).ap()
        sz_dbg = nc.dram_tensor("sz_dbg", [128, DC, T], F32, kind=spk).ap()
        y2_dbg = nc.dram_tensor("y2_dbg", [128, DC, T], F32, kind=spk).ap()
        w1i_dbg = nc.dram_tensor("w1i_dbg", [128, ECD, FF], F32, kind=spk).ap()
        xi_dbg = nc.dram_tensor("xi_dbg", [128, ECD, T], F32, kind=spk).ap()
        q1_dbg = nc.dram_tensor("q1_dbg", [128, 4 * FC], F32, kind=spk).ap()

    cc1_in = nc.dram_tensor("cc1_in", [128, 16], F32, kind="Internal").ap()
    cc1_out = nc.dram_tensor("cc1_out", [128, 16], F32, kind="Internal").ap()
    cc2_in = nc.dram_tensor("cc2_in", [128, 16], F32, kind="Internal").ap()
    cc2_out = nc.dram_tensor("cc2_out", [128, 16], F32, kind="Internal").ap()

    with contextlib.ExitStack() as top:
        e = top.enter_context

        xdbl_sb = e(nc.sbuf_tensor("xdbl_sb", [128, T], BF16))
        A_sb = e(nc.sbuf_tensor("A_sb", [128, DC, NST], F32))
        At_sb = e(nc.sbuf_tensor("At_sb", [128, DC * NST], F32))
        bdt_sb = e(nc.sbuf_tensor("bdt_sb", [128, DC], F32))
        cbv_sb = e(nc.sbuf_tensor("cbv_sb", [128, DC], F32))
        Dv_sb = e(nc.sbuf_tensor("Dv_sb", [128, DC], F32))
        cw_sb = e(nc.sbuf_tensor("cw_sb", [128, DC, 4], F32))
        eye_sb = e(nc.sbuf_tensor("eye_sb", [128, 128], BF16))
        eyef_sb = e(nc.sbuf_tensor("eyef_sb", [128, 128], F32))
        sel_sb = e(nc.sbuf_tensor("sel_sb", [128, 16 * 128], BF16))
        cst_sb = e(nc.sbuf_tensor("cst_sb", [128, 4], F32))
        onesf_sb = e(nc.sbuf_tensor("onesf_sb", [1, 128], F32))
        invD_sb = e(nc.sbuf_tensor("invD_sb", [128, 1], F32))
        carry_sb = e(nc.sbuf_tensor("carry_sb", [128, DC * NST], F32))
        pmax_sb = e(nc.sbuf_tensor("pmax_sb", [128, 2], F32))
        hmax_sb = e(nc.sbuf_tensor("hmax_sb", [128, 2], F32))
        sin1_sb = e(nc.sbuf_tensor("sin1_sb", [128, 1], F32))
        rs1_sb = e(nc.sbuf_tensor("rs1_sb", [128, 1], F32))
        sin2_sb = e(nc.sbuf_tensor("sin2_sb", [128, 1], F32))
        rs2_sb = e(nc.sbuf_tensor("rs2_sb", [128, 1], F32))
        sw1_sb = e(nc.sbuf_tensor("sw1_sb", [128, FC], F32))
        se1_sb = e(nc.sbuf_tensor("se1_sb", [128, FC], F32))
        bq1_sb = e(nc.sbuf_tensor("bq1_sb", [128, FC], F32))
        b1_sb = e(nc.sbuf_tensor("b1_sb", [128, FC], F32))
        sw2_sb = e(nc.sbuf_tensor("sw2_sb", [128, ECD], F32))
        se2_sb = e(nc.sbuf_tensor("se2_sb", [128, ECD], F32))
        bq2_sb = e(nc.sbuf_tensor("bq2_sb", [128, ECD], F32))
        b2_sb = e(nc.sbuf_tensor("b2_sb", [128, ECD], F32))
        tiny_sb = e(nc.sbuf_tensor("tiny_sb", [128, 16], F32))
        stat_sb = e(nc.sbuf_tensor("stat_sb", [1, 8 * 512], F32))

        consts = [1.0, EPSLN, 1.0 / 255.0, EPSQ]
        for i, v in enumerate(consts):
            nc.const_aps.register(F32, v, cst_sb[:, i : i + 1])

        psG = [e(nc.psum_tensor(f"psG{i}", [128, 512], F32)) for i in range(6)]
        psT_g = e(nc.psum_tensor("psT_g", [128, 128], BF16))
        psAR_g = e(nc.psum_tensor("psAR_g", [128, 128], F32))

        sems = {k: Ctr(e(nc.semaphore(k))) for k in ["dS", "dG", "tT", "sS", "vV", "CC", "pP"]}
        block = e(nc.Block())
        P = Prog(nc, block, sems)

        P.op("sync", lambda E: E.dma_start(out=eye_sb[:], in_=eye_i), "dS", 16)
        P.op("sync", lambda E: E.dma_start(out=sel_sb[0:16, :], in_=sel_i), "dS", 16)
        P.op("sync", lambda E: E.dma_start(out=sel_sb[32:48, :], in_=sel_i), "dS", 16)
        Aap = A_sb[:].rearrange("p c n -> p (c n)")
        P.op("gpsimd", lambda E: E.dma_start(
            out=At_sb[:].rearrange("p (c n) -> p c n", n=NST),
            in_=A_log_i.rearrange("(c p) n -> p c n", p=128)), "dG", 16)
        P.op("gpsimd", lambda E: E.dma_start(
            out=bdt_sb[:].rearrange("p (c o) -> p c o", o=1), in_=b_dt_i.rearrange("(c p) o -> p c o", p=128), allow_slow_non_contiguous=True), "dG", 16)
        P.op("gpsimd", lambda E: E.dma_start(
            out=cbv_sb[:].rearrange("p (c o) -> p c o", o=1), in_=conv_b_i.rearrange("(c p) o -> p c o", p=128), allow_slow_non_contiguous=True), "dG", 16)
        P.op("gpsimd", lambda E: E.dma_start(
            out=Dv_sb[:].rearrange("p (c o) -> p c o", o=1), in_=D_i.rearrange("(c p) o -> p c o", p=128), allow_slow_non_contiguous=True), "dG", 16)
        P.op("gpsimd", lambda E: E.dma_start(
            out=cw_sb[:],
            in_=conv_w_i.rearrange("(c p) k -> p c k", p=128)), "dG", 16)
        P.op("gpsimd", lambda E: E.dma_start(
            out=b1_sb[:].rearrange("p (c o) -> p c o", o=1), in_=b1_i.rearrange("(c p) o -> p c o", p=128), allow_slow_non_contiguous=True), "dG", 16)
        P.op("gpsimd", lambda E: E.dma_start(
            out=b2_sb[:].rearrange("p (c o) -> p c o", o=1), in_=b2_i.rearrange("(c p) o -> p c o", p=128), allow_slow_non_contiguous=True), "dG", 16)

        for i, v in enumerate(consts):
            P.op("vector", lambda E, i=i, v=v: E.memset(cst_sb[:, i : i + 1], v))
        P.op("vector", lambda E: E.memset(onesf_sb[:, :], 1.0))
        P.op("vector", lambda E: E.memset(invD_sb[:, :], 1.0 / 1024.0))
        P.op("vector", lambda E: E.memset(carry_sb[:, :], 0.0))
        P.op("vector", lambda E: E.memset(pmax_sb[:, :], -1e30))
        P.op("vector", lambda E: E.memset(hmax_sb[:, :], -1e30))
        P.wait_now("vector", "dS")
        cst_done = P.op("vector", lambda E: E.tensor_copy(eyef_sb[:], eye_sb[:]), "vV")

        P.wait_now("scalar", "dG")
        e1 = P.op("scalar", lambda E: E.activation(Aap, At_sb[:], AF.Exp, scale=1.0), "sS")
        P.wait("scalar", "sS", e1)
        e2 = P.op("scalar", lambda E: E.mul(At_sb[:], Aap, -1.0), "sS")
        P.wait("scalar", "sS", e2)
        P.op("scalar", lambda E: E.copy(Aap, At_sb[:]), "sS")

        P.barrier_all()

        xTv = xT.rearrange("(c p) t -> p c t", p=128)

        if active("M1"):
            with contextlib.ExitStack() as ctx:
                ee = ctx.enter_context
                Wi_sb = ee(nc.sbuf_tensor("Wi_sb", [128, ECD, 2 * DI], BF16))
                Wx_sb = ee(nc.sbuf_tensor("Wx_sb", [128, DC, 128], BF16))
                diag_sb = ee(nc.sbuf_tensor("diag_sb", [128, DC * 4, 128], BF16))
                xk_sb = ee(nc.sbuf_tensor("xk_sb", [128, ECD, L1], BF16))
                xip_sb = ee(nc.sbuf_tensor("xip_sb", [128, DC, L1 + 3], BF16))
                xcv_sb = ee(nc.sbuf_tensor("xcv_sb", [128, DC, L1], BF16))
                szt_sb = ee(nc.sbuf_tensor("szt_sb", [128, DC, L1], BF16))
                psA = [psG[0], psG[1]]
                psC = [psG[2], psG[3]]
                psX = psG[4]

                wload = P.op("gpsimd", lambda E: E.dma_start(
                    out=Wi_sb[:],
                    in_=W_inT.rearrange("(c p) m -> p c m", p=128)), "dG", 16)
                wload = P.op("gpsimd", lambda E: E.dma_start(
                    out=Wx_sb[:],
                    in_=W_xTp.rearrange("(c p) m -> p c m", p=128)), "dG", 16)

                for dc in range(DC):
                    for kk in range(4):
                        last_diag = P.op("vector", lambda E, dc=dc, kk=kk: E.tensor_scalar(
                            diag_sb[:, dc * 4 + kk, :], eye_sb[:],
                            cw_sb[:, dc, kk : kk + 1], None, ALU.mult), "vV")
                for dc in range(DC):
                    P.op("vector", lambda E, dc=dc: E.memset(xip_sb[:, dc, 0:3], 0.0))
                pad_done = P.op("vector", lambda E: E.memset(tiny_sb[:, 0:1], 0.0), "vV")

                evac_done = {}
                conv_evac = {}
                store_done = {}
                xz_mm = {}
                conv_mm = {}
                xload = {}

                for k in range(NT1):
                    ks = slice(k * L1, (k + 1) * L1)
                    P.barrier_all()
                    xload[k] = P.op("gpsimd", lambda E, ks=ks: E.dma_start(
                        out=xk_sb[:],
                        in_=xTv[:, :, ks]), "dG", 16)

                    P.wait("tensor", "dG", xload[k])
                    if k >= 1:
                        P.wait("scalar", "dS", store_done[k - 1])
                    for ec in range(2 * DC):
                        prev = (k, ec - 2) if ec >= 2 else ((k - 1, 2 * DC - 2 + ec) if k >= 1 else None)
                        if prev is not None:
                            P.wait("tensor", "sS", evac_done[prev])
                        for dk in range(ECD):
                            xz_mm[(k, ec)] = P.op(
                                "tensor",
                                lambda E, ec=ec, dk=dk: E.matmul(
                                    psA[ec % 2][:],
                                    Wi_sb[:, dk, ec * 128 : (ec + 1) * 128],
                                    xk_sb[:, dk, :],
                                    start=(dk == 0), stop=(dk == ECD - 1)),
                                "tT" if dk == ECD - 1 else None)
                        P.wait("scalar", "tT", xz_mm[(k, ec)])
                        if ec < DC:
                            dc = ec
                            if k >= 1:
                                P.wait("scalar", "tT", conv_mm[(k - 1, dc)])
                                cce = P.op("scalar", lambda E, dc=dc: E.copy(
                                    xip_sb[:, dc, 0:3], xip_sb[:, dc, L1 : L1 + 3]), "sS")
                                P.wait("scalar", "sS", cce)
                            evac_done[(k, ec)] = P.op(
                                "scalar", lambda E, dc=dc, ec=ec: E.copy(
                                    xip_sb[:, dc, 3 : L1 + 3], psA[ec % 2][:]), "sS")
                        else:
                            evac_done[(k, ec)] = P.op(
                                "scalar", lambda E, ec=ec: E.activation(
                                    szt_sb[:, ec - DC, :], psA[ec % 2][:], AF.Silu), "sS")

                    P.wait("tensor", "vV", pad_done)
                    for dc in range(DC):
                        P.wait("tensor", "sS", evac_done[(k, dc)])
                        prev = (k, dc - 2) if dc >= 2 else ((k - 1, DC - 2 + dc) if k >= 1 else None)
                        if prev is not None:
                            P.wait("tensor", "sS", conv_evac[prev])
                        for kk in range(4):
                            conv_mm[(k, dc)] = P.op(
                                "tensor", lambda E, dc=dc, kk=kk: E.matmul(
                                    psC[dc % 2][:], diag_sb[:, dc * 4 + kk, :],
                                    xip_sb[:, dc, kk : kk + L1],
                                    start=(kk == 0), stop=(kk == 3)),
                                "tT" if kk == 3 else None)
                        P.wait("scalar", "tT", conv_mm[(k, dc)])
                        conv_evac[(k, dc)] = P.op(
                            "scalar", lambda E, dc=dc: E.activation(
                                xcv_sb[:, dc, :], psC[dc % 2][:], AF.Silu,
                                bias=cbv_sb[:, dc : dc + 1]), "sS")

                    P.wait("tensor", "sS", conv_evac[(k, DC - 1)])
                    for kdc in range(DC):
                        xdbl_mm = P.op("tensor", lambda E, kdc=kdc: E.matmul(
                            psX[:], Wx_sb[:, kdc, :], xcv_sb[:, kdc, :],
                            start=(kdc == 0), stop=(kdc == DC - 1)),
                            "tT" if kdc == DC - 1 else None)
                    P.wait("scalar", "tT", xdbl_mm)
                    xdbl_evac = P.op("scalar", lambda E, ks=ks: E.copy(
                        xdbl_sb[:, ks], psX[:]), "sS")

                    P.wait("sync", "sS", conv_evac[(k, DC - 1)])
                    P.op("sync", lambda E, ks=ks: E.dma_start(
                        out=xconv_sp[:, :, ks],
                        in_=xcv_sb[:]), "dS", 16)
                    P.wait("sync", "sS", evac_done[(k, 2 * DC - 1)])
                    store_done[k] = P.op("sync", lambda E, ks=ks: E.dma_start(
                        out=sz_sp[:, :, ks],
                        in_=szt_sb[:]), "dS", 16)

                P.barrier_all()

        if active("M2"):
            with contextlib.ExitStack() as ctx:
                ee = ctx.enter_context
                Wdt_sb = ee(nc.sbuf_tensor("Wdt_sb", [128, DC, 128], BF16))
                xcv2_sb = ee(nc.sbuf_tensor("xcv2_sb", [128, DC, L1], BF16))
                szt2_sb = ee(nc.sbuf_tensor("szt2_sb", [128, DC, L1], BF16))
                dt_sb = ee(nc.sbuf_tensor("dt_sb", [128, DC, L1], F32))
                dte_sb = ee(nc.sbuf_tensor("dte_sb", [128, L1], F32))
                u_sb = ee(nc.sbuf_tensor("u_sb", [128, DC, L1], BF16))
                Bb_sb = ee(nc.sbuf_tensor("Bb_sb", [128, NST, L1], BF16))
                Cb_sb = ee(nc.sbuf_tensor("Cb_sb", [128, NST, L1], BF16))
                y2_sb = ee(nc.sbuf_tensor("y2_sb", [128, DC, L1], BF16))
                NB = 4
                dA_sb = [ee(nc.sbuf_tensor(f"dA{i}", [128, L1], F32)) for i in range(NB)]
                Zt_sb = [ee(nc.sbuf_tensor(f"Zt{i}", [128, L1], BF16)) for i in range(NB)]
                st_sb = [ee(nc.sbuf_tensor(f"st{i}", [128, L1], BF16)) for i in range(NB)]
                Pt_sb = [ee(nc.sbuf_tensor(f"Pt{i}", [128, L1], BF16)) for i in range(NB)]
                tg_sb = ee(nc.sbuf_tensor("tg_sb", [128, L1], F32))
                psD = [psG[0], psG[1]]
                psBC = [psG[2], psG[3]]
                psY = [psG[4], psG[5]]

                P.op("gpsimd", lambda E: E.dma_start(
                    out=Wdt_sb[64:128, :, :],
                    in_=W_dtT_i.rearrange("k (c m) -> k c m", m=128)), "dG", 16)

                dt_ev = {}
                bc_ev = {}
                u_done = {}
                dA_done = {}
                zi_done = {}
                scan_done = {}
                carry_done = {}
                scanP = {}
                ymm = {}
                gat = {}
                ld_done = {}
                st_done = {}

                for k in range(NT1):
                    ks = slice(k * L1, (k + 1) * L1)
                    if k == 0:
                        P.barrier_all()
                    else:
                        P.wait("sync", "vV", gat[(k - 1, DC - 1)])
                    P.op("sync", lambda E, ks=ks: E.dma_start(
                        out=xcv2_sb[:],
                        in_=xconv_sp[:, :, ks]), "dS", 16)
                    ld_done[k] = P.op("sync", lambda E, ks=ks: E.dma_start(
                        out=szt2_sb[:],
                        in_=sz_sp[:, :, ks]), "dS", 16)

                    P.wait_now("tensor", "dG")
                    if k >= 1:
                        pass
                    for dc in range(DC):
                        if dc >= 2:
                            P.wait("tensor", "sS", dt_ev[(k, dc - 2)])
                        elif k >= 1:
                            P.wait("tensor", "sS", dt_ev[(k - 1, DC - 2 + dc)])
                        dtmm = P.op("tensor", lambda E, dc=dc, ks=ks: E.matmul(
                            psD[dc % 2][:], Wdt_sb[64:128, dc, :], xdbl_sb[64:128, ks],
                            start=True, stop=True), "tT")
                        P.wait("scalar", "tT", dtmm)
                        if k >= 1:
                            P.wait("scalar", "pP" if zpool else "vV",
                                   u_done[(k - 1, dc)])
                        if (k, dc - 1) in dt_ev:
                            P.wait("scalar", "sS", dt_ev[(k, dc - 1)])
                        de = P.op("scalar", lambda E, dc=dc: E.activation(
                            dte_sb[:], psD[dc % 2][:], AF.Exp,
                            bias=bdt_sb[:, dc : dc + 1]), "sS")
                        P.wait("scalar", "sS", de)
                        dt_ev[(k, dc)] = P.op("scalar", lambda E, dc=dc: E.activation(
                            dt_sb[:, dc, :], dte_sb[:], AF.Ln, bias=1.0), "sS")

                    if k >= 1:
                        P.wait("scalar", "vV", gat[(k - 1, DC - 1)])
                    for n in range(NST):
                        for which, src_lo, selbase, dst in (
                                (0, 0, 0, Bb_sb), (1, 32, 32, Cb_sb)):
                            idx = n * 2 + which
                            if idx >= 2:
                                P.wait("tensor", "sS", bc_ev[(k, idx - 2)])
                            elif k >= 1:
                                P.wait("tensor", "sS", bc_ev[(k - 1, 2 * NST - 2 + idx)])
                            bcm = P.op("tensor", lambda E, n=n, src_lo=src_lo,
                                       selbase=selbase, ks=ks, idx=idx: E.matmul(
                                           psBC[idx % 2][:],
                                           sel_sb[selbase : selbase + 16,
                                                  n * 128 : (n + 1) * 128],
                                           xdbl_sb[src_lo : src_lo + 16, ks],
                                           start=True, stop=True), "tT")
                            P.wait("scalar", "tT", bcm)
                            bc_ev[(k, idx)] = P.op(
                                "scalar", lambda E, n=n, dst=dst, idx=idx: E.copy(
                                    dst[:, n, :], psBC[idx % 2][:]), "sS")

                    ZENG = "gpsimd" if zpool else "vector"
                    ZSEM = "pP" if zpool else "vV"
                    P.wait(ZENG, "dS", ld_done[k])
                    for dc in range(DC):
                        P.wait(ZENG, "sS", dt_ev[(k, dc)])
                        u_done[(k, dc)] = P.op(ZENG, lambda E, dc=dc: E.tensor_tensor(
                            u_sb[:, dc, :], dt_sb[:, dc, :], xcv2_sb[:, dc, :],
                            ALU.mult), ZSEM)

                    if k >= 1:
                        P.wait("vector", "dS", st_done[k - 1])
                    step = 0
                    for dc in range(DC):
                        for n in range(NST):
                            i2 = step % 2
                            iz = step % NB
                            cidx = dc * NST + n
                            if step >= 2:
                                P.wait("scalar", "vV", scanP[(k, step - 2)])
                            elif k >= 1:
                                P.wait("scalar", "vV", scanP[(k - 1, 254 + step)])
                            dA_done[(k, step)] = P.op(
                                "scalar", lambda E, dc=dc, n=n, i2=i2: E.activation(
                                    dA_sb[i2][:], dt_sb[:, dc, :], AF.Exp,
                                    scale=A_sb[:, dc, n : n + 1]), "sS")
                            P.wait(ZENG, "sS", bc_ev[(k, 2 * n)])
                            prevz = (k, step - NB) if step >= NB else (
                                (k - 1, 256 - NB + step) if k >= 1 else None)
                            if prevz is not None:
                                P.wait(ZENG, "vV", scan_done[prevz])
                            zi_done[(k, step)] = P.op(
                                ZENG, lambda E, dc=dc, n=n, iz=iz:
                                E.tensor_tensor(
                                    Zt_sb[iz][:], u_sb[:, dc, :],
                                    Bb_sb[:, n, :], ALU.mult), ZSEM)
                            P.wait("vector", "sS", dA_done[(k, step)])
                            if zpool:
                                P.wait("vector", "pP", zi_done[(k, step)])
                            if k == 0:
                                si = P.op("vector", lambda E, i2=i2, iz=iz:
                                          E.tensor_tensor_scan(
                                              st_sb[i2][:], dA_sb[i2][:],
                                              Zt_sb[iz][:],
                                              0.0, ALU.mult, ALU.add), "vV")
                            else:
                                si = P.op("vector", lambda E, i2=i2, iz=iz,
                                          cidx=cidx:
                                          E.tensor_tensor_scan(
                                              st_sb[i2][:], dA_sb[i2][:],
                                              Zt_sb[iz][:],
                                              carry_sb[:, cidx : cidx + 1],
                                              ALU.mult, ALU.add), "vV")
                            scan_done[(k, step)] = si
                            P.wait("vector", "vV", si)
                            P.op("vector", lambda E, i2=i2, cidx=cidx: E.tensor_copy(
                                carry_sb[:, cidx : cidx + 1],
                                st_sb[i2][:, L1 - 1 : L1]), "vV")
                            if step >= 2:
                                P.wait("vector", "tT", ymm[(k, step - 2)])
                            elif k >= 1:
                                P.wait("vector", "tT", ymm[(k - 1, 254 + step)])
                            P.wait("vector", "sS", bc_ev[(k, 2 * n + 1)])
                            scanP[(k, step)] = P.op(
                                "vector", lambda E, i2=i2, n=n: E.tensor_tensor(
                                    Pt_sb[i2][:], st_sb[i2][:], Cb_sb[:, n, :],
                                    ALU.mult), "vV")
                            P.wait("tensor", "vV", scanP[(k, step)])
                            if n == 0 and dc >= 2:
                                P.wait("tensor", "vV", gat[(k, dc - 2)])
                            elif n == 0 and k >= 1 and dc < 2:
                                P.wait("tensor", "vV", gat[(k - 1, DC - 2 + dc)])
                            ymm[(k, step)] = P.op(
                                "tensor", lambda E, i2=i2, dc=dc, n=n:
                                E.matmul(psY[dc % 2][:], eye_sb[:],
                                         Pt_sb[i2][:],
                                         start=(n == 0), stop=(n == NST - 1)),
                                "tT")
                            step += 1
                        P.wait("vector", "tT", ymm[(k, step - 1)])
                        tgi = P.op("vector", lambda E, dc=dc: E.tensor_tensor(
                            tg_sb[:], psY[dc % 2][:], szt2_sb[:, dc, :], ALU.mult), "vV")
                        P.wait("vector", "vV", tgi)
                        gat[(k, dc)] = P.op(
                            "vector", lambda E, dc=dc: E.scalar_tensor_tensor(
                                y2_sb[:, dc, :], xcv2_sb[:, dc, :],
                                Dv_sb[:, dc : dc + 1], tg_sb[:],
                                ALU.mult, ALU.add), "vV")

                    P.wait("sync", "vV", gat[(k, DC - 1)])
                    st_done[k] = P.op("sync", lambda E, ks=ks: E.dma_start(
                        out=y2_sp[:, :, ks],
                        in_=y2_sb[:]), "dS", 16)
                    if debug_spills:
                        P.wait("sync", "sS", dt_ev[(k, DC - 1)])
                        dbg_done = P.op("sync", lambda E, ks=ks: E.dma_start(
                            out=dbg_sp[:, :, ks],
                            in_=dt_sb[:]), "dS", 16)
                        st_done[k] = dbg_done
                        P.wait("scalar", "dS", dbg_done)

                P.barrier_all()

        def layer_norm_tiles(Ltile, v_sb, vsq_sb, x1out_sb, psm, psq, psB, necd=ECD):
            mean_s = stat_sb[:, 0:Ltile]
            msq_s = stat_sb[:, 512 : 512 + Ltile]
            var_s = stat_sb[:, 1024 : 1024 + Ltile]
            rstd_s = stat_sb[:, 1536 : 1536 + Ltile]
            mrstd_s = stat_sb[:, 2048 : 2048 + Ltile]
            tmp_s = stat_sb[:, 2560 : 2560 + Ltile]
            tmp2_s = stat_sb[:, 2560 + 512 : 2560 + 512 + Ltile]
            sq = P.op("scalar", lambda E: E.activation(
                vsq_sb[:].rearrange("p c l -> p (c l)"),
                v_sb[:].rearrange("p c l -> p (c l)"), AF.Square), "sS")
            P.wait("tensor", "sS", sq)
            for ec in range(necd):
                P.op("tensor", lambda E, ec=ec: E.matmul(
                    psm[0:1, :Ltile], invD_sb[:], v_sb[:, ec, :],
                    start=(ec == 0), stop=(ec == necd - 1)))
            for ec in range(necd):
                smm = P.op("tensor", lambda E, ec=ec: E.matmul(
                    psq[0:1, :Ltile], invD_sb[:], vsq_sb[:, ec, :],
                    start=(ec == 0), stop=(ec == necd - 1)),
                    "tT" if ec == necd - 1 else None)
            P.wait("scalar", "tT", smm)
            P.op("scalar", lambda E: E.copy(mean_s, psm[0:1, :Ltile]))
            ecop = P.op("scalar", lambda E: E.copy(msq_s, psq[0:1, :Ltile]), "sS")
            P.wait("vector", "sS", ecop)
            tmpi = P.op("vector", lambda E: E.tensor_tensor(
                tmp_s, mean_s, mean_s, ALU.mult), "vV")
            P.wait("vector", "vV", tmpi)
            vv = P.op("vector", lambda E: E.tensor_tensor(
                var_s, msq_s, tmp_s, ALU.subtract), "vV")
            P.wait("scalar", "vV", vv)
            lni = P.op("scalar", lambda E: E.activation(
                tmp2_s, var_s, AF.Ln, bias=EPSLN), "sS")
            P.wait("scalar", "sS", lni)
            rr = P.op("scalar", lambda E: E.activation(
                rstd_s, tmp2_s, AF.Exp, scale=-0.5), "sS")
            P.wait("vector", "sS", rr)
            mm = P.op("vector", lambda E: E.tensor_tensor(
                mrstd_s, mean_s, rstd_s, ALU.mult), "vV")
            P.wait("tensor", "vV", mm)
            P.op("tensor", lambda E: E.matmul(
                psB[0][0:128, :Ltile], onesf_sb[:], rstd_s, start=True, stop=True))
            bb = P.op("tensor", lambda E: E.matmul(
                psB[1][0:128, :Ltile], onesf_sb[:], mrstd_s, start=True, stop=True), "tT")
            P.wait("vector", "tT", bb)
            for ec in range(necd):
                mi = P.op("vector", lambda E, ec=ec: E.tensor_tensor(
                    vsq_sb[:, ec, :], v_sb[:, ec, :], psB[0][0:128, :Ltile], ALU.mult), "vV")
                P.wait("vector", "vV", mi)
                lnv = P.op("vector", lambda E, ec=ec: E.tensor_tensor(
                    x1out_sb[:, ec, :], vsq_sb[:, ec, :], psB[1][0:128, :Ltile],
                    ALU.subtract), "vV")
            return lnv

        if active("M3"):
            with contextlib.ExitStack() as ctx:
                ee = ctx.enter_context
                Wo_sb = ee(nc.sbuf_tensor("Wo_sb", [128, DC, D], BF16))
                y2l_sb = ee(nc.sbuf_tensor("y2l_sb", [128, DC, L1], BF16))
                xf_sb = ee(nc.sbuf_tensor("xf_sb", [128, ECD, L1], F32))
                v_sb = ee(nc.sbuf_tensor("v_sb", [128, ECD, L1], F32))
                vsq_sb = ee(nc.sbuf_tensor("vsq_sb", [128, ECD, L1], F32))
                x1_sb = ee(nc.sbuf_tensor("x1_sb", [128, ECD, L1], F32))
                psO = [psG[0], psG[1]]
                psm = psG[2]
                psq = psG[3]
                psB = [psG[4], psG[5]]

                P.op("gpsimd", lambda E: E.dma_start(
                    out=Wo_sb[:],
                    in_=W_outT_i.rearrange("(c p) m -> p c m", p=128)), "dG", 16)

                stx = None
                for k in range(NT1):
                    ks = slice(k * L1, (k + 1) * L1)
                    P.barrier_all()
                    P.op("sync", lambda E, ks=ks: E.dma_start(
                        out=y2l_sb[:],
                        in_=y2_sp[:, :, ks]), "dS", 16)
                    ldk = P.op("sync", lambda E, ks=ks: E.dma_start(
                        out=xf_sb[:],
                        in_=xTv[:, :, ks]), "dS", 16)
                    P.wait("tensor", "dS", ldk)
                    P.wait_now("tensor", "dG")
                    vadds = {}
                    for ec in range(ECD):
                        if ec >= 2:
                            P.wait("tensor", "vV", vadds[ec - 2])
                        for kdc in range(DC):
                            omm = P.op("tensor", lambda E, ec=ec, kdc=kdc: E.matmul(
                                psO[ec % 2][:], Wo_sb[:, kdc, ec * 128 : (ec + 1) * 128],
                                y2l_sb[:, kdc, :],
                                start=(kdc == 0), stop=(kdc == DC - 1)),
                                "tT" if kdc == DC - 1 else None)
                        P.wait("vector", "tT", omm)
                        vadds[ec] = P.op("vector", lambda E, ec=ec: E.tensor_tensor(
                            v_sb[:, ec, :], psO[ec % 2][:], xf_sb[:, ec, :],
                            ALU.add), "vV")
                    P.wait("scalar", "vV", vadds[ECD - 1])
                    P.wait("tensor", "vV", vadds[ECD - 1])
                    lnv = layer_norm_tiles(L1, v_sb, vsq_sb, x1_sb, psm, psq, psB)
                    P.wait("vector", "vV", lnv)
                    ri = P.op("vector", lambda E: E.tensor_reduce(
                        tiny_sb[:, 0:1], x1_sb[:].rearrange("p c l -> p (c l)"),
                        AX.X, ALU.max), "vV")
                    P.wait("vector", "vV", ri)
                    a, b = k % 2, (k + 1) % 2
                    pm = P.op("vector", lambda E, a=a, b=b: E.tensor_tensor(
                        pmax_sb[:, b : b + 1], pmax_sb[:, a : a + 1],
                        tiny_sb[:, 0:1], ALU.max), "vV")
                    P.wait("sync", "vV", lnv)
                    stx = P.op("sync", lambda E, ks=ks: E.dma_start(
                        out=x1_sp[:, :, ks],
                        in_=x1_sb[:]), "dS", 16)
                    P.wait("sync", "vV", pm)
                    P.wait("sync", "dS", stx)

                P.barrier_all()

        def allreduce_max(maxv_sb, sinv_sb, rsv_sb, ccin, ccout, psB0):
            tr = P.op("tensor", lambda E: E.transpose(
                psB0[0:1, 0:128], maxv_sb[:], eyef_sb[:]), "tT")
            P.wait("vector", "tT", tr)
            e1 = P.op("vector", lambda E: E.reduce_max(
                tiny_sb[0:1, 0:1], psB0[0:1, 0:128], AX.X), "vV")
            P.wait("vector", "vV", e1)
            gm = P.op("vector", lambda E: E.tensor_scalar(
                tiny_sb[0:1, 1:2], tiny_sb[0:1, 0:1], EPSQ, 1.0 / 255.0,
                ALU.max, ALU.mult), "vV")
            P.wait("tensor", "vV", gm)
            bc = P.op("tensor", lambda E: E.matmul(
                psB0[0:128, 0:1], onesf_sb[:], tiny_sb[0:1, 1:2],
                start=True, stop=True), "tT")
            P.wait("scalar", "tT", bc)
            sc = P.op("scalar", lambda E: E.copy(sinv_sb[:], psB0[0:128, 0:1]), "sS")
            P.wait("vector", "sS", sc)
            b16, _ = bass.broadcast_tensor_aps(sinv_sb[:, 0:1], tiny_sb[:, 0:16])
            vb = P.op("vector", lambda E, b16=b16: E.tensor_copy(
                tiny_sb[:, 0:16], b16), "vV")
            P.wait("gpsimd", "vV", vb)
            dg1 = P.op("gpsimd", lambda E: E.dma_start(
                out=ccin, in_=tiny_sb[:, 0:16]), "dG", 16)
            P.wait("gpsimd", "dG", dg1)
            cc = P.op("gpsimd", lambda E: E.collective_compute(
                "AllReduce", ALU.max,
                replica_groups=[[0, 1, 2, 3, 4, 5, 6, 7]],
                ins=[ccin], outs=[ccout]), "CC")
            P.wait("gpsimd", "CC", cc)
            dg2 = P.op("gpsimd", lambda E: E.dma_start(
                out=tiny_sb[:, 0:16], in_=ccout), "dG", 16)
            P.wait("vector", "dG", dg2)
            e2 = P.op("vector", lambda E: E.tensor_copy(sinv_sb[:], tiny_sb[:, 0:1]), "vV")
            P.wait("vector", "vV", e2)
            rv = P.op("vector", lambda E: E.reciprocal(rsv_sb[:], sinv_sb[:]), "vV")
            return rv

        if active("AR1"):
            allreduce_max(pmax_sb[:, 0:1], sin1_sb, rs1_sb, cc1_in, cc1_out, psAR_g)
            P.barrier_all()

        def wq_phase(win, nchunk, width, sw_sb, wi_sb, nd_out, b_sb, se_sb, bq_sb,
                     sin_sb, name):
            with contextlib.ExitStack() as ctx:
                ee = ctx.enter_context
                wch = [ee(nc.sbuf_tensor(f"wch{i}_{name}", [128, width], F32))
                       for i in range(2)]
                q32 = ee(nc.sbuf_tensor(f"q32_{name}", [128, width], I32))
                qbf = ee(nc.sbuf_tensor(f"qbf_{name}", [128, width], BF16))
                rw = ee(nc.sbuf_tensor(f"rw_{name}", [128, 1], F32))
                tq = ee(nc.sbuf_tensor(f"tq_{name}", [128, width], F32))
                q32b = ee(nc.sbuf_tensor(f"q32b_{name}", [128, nchunk], I32))
                tmpb = ee(nc.sbuf_tensor(f"tmpb_{name}", [128, nchunk], F32))
                tmpb2 = ee(nc.sbuf_tensor(f"tmpb2_{name}", [128, nchunk], F32))
                psT = psT_g
                winv = win.rearrange("(c p) d -> p c d", p=128)
                ndc = width // 128
                P.barrier_all()
                lds = {}
                e4s = {}
                for fcx in range(nchunk):
                    w2 = fcx % 2
                    if fcx == 0:
                        lds[0] = P.op("sync", lambda E: E.dma_start(
                            out=wch[0][:], in_=winv[:, 0, :]), "dS", 16)
                    if fcx + 1 < nchunk:
                        if fcx >= 1:
                            P.wait("sync", "vV", e4s[fcx - 1])
                        lds[fcx + 1] = P.op("sync", lambda E, fcx=fcx, w2=w2:
                                            E.dma_start(out=wch[1 - w2][:],
                                                        in_=winv[:, fcx + 1, :]),
                                            "dS", 16)
                    P.wait("vector", "dS", lds[fcx])
                    if fcx >= 1:
                        P.wait("vector", "sS", trev)
                    e1 = P.op("vector", lambda E, w2=w2: E.tensor_reduce(
                        tiny_sb[:, 4:5], wch[w2][:], AX.X, ALU.max,
                        apply_absolute_value=True), "vV")
                    P.wait("vector", "vV", e1)
                    e2 = P.op("vector", lambda E, fcx=fcx: E.tensor_scalar(
                        sw_sb[:, fcx : fcx + 1], tiny_sb[:, 4:5],
                        EPSQ, 1.0 / 127.0, ALU.max, ALU.mult), "vV")
                    P.wait("vector", "vV", e2)
                    e3 = P.op("vector", lambda E, fcx=fcx: E.reciprocal(
                        rw[:], sw_sb[:, fcx : fcx + 1]), "vV")
                    P.wait("vector", "vV", e3)
                    e4s[fcx] = P.op("vector", lambda E, w2=w2: E.tensor_scalar(
                        tq[:], wch[w2][:], rw[:, 0:1], None, ALU.mult), "vV")
                    P.wait("vector", "vV", e4s[fcx])
                    e4c = P.op("vector", lambda E: E.tensor_copy(
                        q32[:], tq[:]), "vV")
                    P.wait("vector", "vV", e4c)
                    qdone = P.op("vector", lambda E: E.tensor_scalar(
                        qbf[:], q32[:], -128.0, 127.0, ALU.max, ALU.min), "vV")
                    P.wait("tensor", "vV", qdone)
                    for dcc in range(ndc):
                        if dcc >= 1 or fcx >= 1:
                            P.wait("tensor", "sS", trev)
                        trm = P.op("tensor", lambda E, dcc=dcc: E.transpose(
                            psT[:], qbf[:, dcc * 128 : (dcc + 1) * 128],
                            eye_sb[:]), "tT")
                        P.wait("scalar", "tT", trm)
                        trev = P.op("scalar", lambda E, dcc=dcc, fcx=fcx: E.copy(
                            wi_sb[:, dcc, fcx * 128 : (fcx + 1) * 128], psT[:]), "sS")
                e1 = P.op("vector", lambda E: E.tensor_scalar(
                    se_sb[:], sw_sb[:], sin_sb[:, 0:1], None, ALU.mult), "vV")
                P.wait("vector", "vV", e1)
                e2 = P.op("vector", lambda E: E.reciprocal(tmpb[:], se_sb[:]), "vV")
                P.wait("vector", "vV", e2)
                e3 = P.op("vector", lambda E: E.tensor_tensor(
                    tmpb2[:], b_sb[:], tmpb[:], ALU.mult), "vV")
                P.wait("vector", "vV", e3)
                e4 = P.op("vector", lambda E: E.tensor_copy(
                    q32b[:], tmpb2[:]), "vV")
                P.wait("vector", "vV", e4)
                bqd = P.op("vector", lambda E: E.tensor_tensor(
                    bq_sb[:], q32b[:], se_sb[:], ALU.mult), "vV")
                return bqd

        if active("WQ1"):
            wq1_ctx = top.enter_context(contextlib.ExitStack())
            w1i_sb = wq1_ctx.enter_context(
                nc.sbuf_tensor("w1i_sb", [128, ECD, FF], BF16))
            wq_phase(w1n_i, FC, D, sw1_sb, w1i_sb, ECD, b1_sb, se1_sb, bq1_sb,
                     sin1_sb, "w1")
            P.barrier_all()
            if debug_spills:
                P.op("gpsimd", lambda E: E.dma_start(out=w1i_dbg, in_=w1i_sb[:]), "dG", 16)
                P.op("gpsimd", lambda E: E.dma_start(out=q1_dbg[:, 0:FC], in_=sw1_sb[:]), "dG", 16)
                P.op("gpsimd", lambda E: E.dma_start(out=q1_dbg[:, FC:2 * FC], in_=se1_sb[:]), "dG", 16)
                P.op("gpsimd", lambda E: E.dma_start(out=q1_dbg[:, 2 * FC:3 * FC], in_=bq1_sb[:]), "dG", 16)
                P.op("gpsimd", lambda E: E.dma_start(out=q1_dbg[:, 3 * FC:3 * FC + 1], in_=sin1_sb[:], allow_slow_non_contiguous=True), "dG", 16)
                P.wait_now("gpsimd", "dG")
                P.barrier_all()

        if active("F1"):
            with contextlib.ExitStack() as ctx:
                ee = ctx.enter_context
                x1l_sb = ee(nc.sbuf_tensor("x1l_sb", [128, ECD, L1], F32))
                xi32_sb = ee(nc.sbuf_tensor("xi32_sb", [128, L1], I32))
                xi_sb = ee(nc.sbuf_tensor("xi_sb", [128, ECD, L1], BF16))
                h_sb = ee(nc.sbuf_tensor("h_sb", [128, FC, L1], F32))
                psH = [psG[0], psG[1]]

                hev = {}
                for k in range(NT1):
                    ks = slice(k * L1, (k + 1) * L1)
                    P.barrier_all()
                    ld = P.op("sync", lambda E, ks=ks: E.dma_start(
                        out=x1l_sb[:],
                        in_=x1_sp[:, :, ks]), "dS", 16)
                    P.wait("vector", "dS", ld)
                    for ec in range(ECD):
                        if k >= 1 and ec == 0:
                            P.wait("vector", "tT", hmm_last)
                        if ec >= 1:
                            P.wait("vector", "vV", qd)
                        q32i = P.op("vector", lambda E, ec=ec: E.tensor_scalar(
                            xi32_sb[:], x1l_sb[:, ec, :], rs1_sb[:, 0:1], None,
                            ALU.mult), "vV")
                        P.wait("vector", "vV", q32i)
                        qd = P.op("vector", lambda E, ec=ec: E.tensor_scalar(
                            xi_sb[:, ec, :], xi32_sb[:], 0.0, 255.0,
                            ALU.max, ALU.min), "vV")
                    P.wait("tensor", "vV", qd)
                    if debug_spills:
                        P.wait("gpsimd", "vV", qd)
                        P.op("gpsimd", lambda E, ks=ks: E.dma_start(
                            out=xi_dbg[:, :, ks], in_=xi_sb[:]), "dG", 16)
                    for fcx in range(FC):
                        prev = fcx - 2 if fcx >= 2 else (FC - 2 + fcx if k >= 1 else None)
                        if prev is not None and (k if fcx >= 2 else k - 1, prev) in hev:
                            P.wait("tensor", "sS",
                                   hev[(k if fcx >= 2 else k - 1, prev)])
                        for kd in range(ECD):
                            hmm_last = P.op("tensor", lambda E, fcx=fcx, kd=kd: E.matmul(
                                psH[fcx % 2][:],
                                w1i_sb[:, kd, fcx * 128 : (fcx + 1) * 128],
                                xi_sb[:, kd, :],
                                start=(kd == 0), stop=(kd == ECD - 1)),
                                "tT" if kd == ECD - 1 else None)
                        P.wait("scalar", "tT", hmm_last)
                        if k >= 1 and fcx == 0:
                            P.wait("scalar", "dS", hst)
                            P.wait("scalar", "vV", hm)
                        hev[(k, fcx)] = P.op("scalar", lambda E, fcx=fcx: E.activation(
                            h_sb[:, fcx, :], psH[fcx % 2][:], AF.Relu,
                            scale=se1_sb[:, fcx : fcx + 1],
                            bias=bq1_sb[:, fcx : fcx + 1]), "sS")
                    P.wait("vector", "sS", hev[(k, FC - 1)])
                    ri = P.op("vector", lambda E: E.tensor_reduce(
                        tiny_sb[:, 2:3], h_sb[:].rearrange("p c l -> p (c l)"),
                        AX.X, ALU.max), "vV")
                    P.wait("vector", "vV", ri)
                    a, b = k % 2, (k + 1) % 2
                    hm = P.op("vector", lambda E, a=a, b=b: E.tensor_tensor(
                        hmax_sb[:, b : b + 1], hmax_sb[:, a : a + 1],
                        tiny_sb[:, 2:3], ALU.max), "vV")
                    P.wait("sync", "sS", hev[(k, FC - 1)])
                    hst = P.op("sync", lambda E, ks=ks: E.dma_start(
                        out=h_sp[:, :, ks],
                        in_=h_sb[:]), "dS", 16)
                    P.wait("sync", "vV", hm)

                P.barrier_all()
            wq1_ctx.close()

        if active("AR2"):
            allreduce_max(hmax_sb[:, 0:1], sin2_sb, rs2_sb, cc2_in, cc2_out, psAR_g)
            P.barrier_all()

        if active("WQ2"):
            wq2_ctx = top.enter_context(contextlib.ExitStack())
            w2i_sb = wq2_ctx.enter_context(
                nc.sbuf_tensor("w2i_sb", [128, FC, D], BF16))
            wq_phase(w2n_i, ECD, FF, sw2_sb, w2i_sb, FC, b2_sb, se2_sb, bq2_sb,
                     sin2_sb, "w2")
            P.barrier_all()

        if active("F2"):
            with contextlib.ExitStack() as ctx:
                ee = ctx.enter_context
                h2_sb = ee(nc.sbuf_tensor("h2_sb", [128, FC, L2], F32))
                hi32_sb = ee(nc.sbuf_tensor("hi32_sb", [128, L2], I32))
                hi_sb = ee(nc.sbuf_tensor("hi_sb", [128, FC, L2], BF16))
                x12_sb = ee(nc.sbuf_tensor("x12_sb", [128, ECD, L2], F32))
                ffn_sb = ee(nc.sbuf_tensor("ffn_sb", [128, ECD, L2], F32))
                v2_sb = ee(nc.sbuf_tensor("v2_sb", [128, ECD, L2], F32))
                vsq2_sb = ee(nc.sbuf_tensor("vsq2_sb", [128, ECD, L2], F32))
                x2_sb = ee(nc.sbuf_tensor("x2_sb", [128, ECD, L2], F32))
                psO2 = [psG[0][:, 0:L2], psG[1][:, 0:L2]]
                psm2 = psG[2]
                psq2 = psG[3]
                psB2 = [psG[4], psG[5]]

                for k in range(NT2):
                    ks = slice(k * L2, (k + 1) * L2)
                    P.barrier_all()
                    P.op("sync", lambda E, ks=ks: E.dma_start(
                        out=h2_sb[:],
                        in_=h_sp[:, :, ks]), "dS", 16)
                    ld2 = P.op("sync", lambda E, ks=ks: E.dma_start(
                        out=x12_sb[:],
                        in_=x1_sp[:, :, ks]), "dS", 16)
                    P.wait("vector", "dS", ld2)
                    qd2_done = {}
                    for fcx in range(FC):
                        if k >= 1 and fcx == 0:
                            P.wait("vector", "tT", omm2)
                        if fcx >= 1:
                            P.wait("vector", "vV", qd2_done[fcx - 1])
                        q32i = P.op("vector", lambda E, fcx=fcx: E.tensor_scalar(
                            hi32_sb[:], h2_sb[:, fcx, :], rs2_sb[:, 0:1], None,
                            ALU.mult), "vV")
                        P.wait("vector", "vV", q32i)
                        qd2_done[fcx] = P.op("vector", lambda E, fcx=fcx: E.tensor_scalar(
                            hi_sb[:, fcx, :], hi32_sb[:], 0.0, 255.0,
                            ALU.max, ALU.min), "vV")
                    qd2 = qd2_done[FC - 1]
                    ev2 = {}
                    for ec in range(ECD):
                        if ec >= 2:
                            P.wait("tensor", "sS", ev2[ec - 2])
                        for kf in range(FC):
                            if ec == 0:
                                P.wait("tensor", "vV", qd2_done[kf])
                            omm2 = P.op("tensor", lambda E, ec=ec, kf=kf: E.matmul(
                                psO2[ec % 2],
                                w2i_sb[:, kf, ec * 128 : (ec + 1) * 128],
                                hi_sb[:, kf, :],
                                start=(kf == 0), stop=(kf == FC - 1)),
                                "tT" if kf == FC - 1 else None)
                        P.wait("scalar", "tT", omm2)
                        ev2[ec] = P.op("scalar", lambda E, ec=ec: E.activation(
                            ffn_sb[:, ec, :], psO2[ec % 2], AF.Identity,
                            scale=se2_sb[:, ec : ec + 1],
                            bias=bq2_sb[:, ec : ec + 1]), "sS")
                    for ec in range(ECD):
                        P.wait("vector", "sS", ev2[ec])
                        va2 = P.op("vector", lambda E, ec=ec: E.tensor_tensor(
                            v2_sb[:, ec, :], ffn_sb[:, ec, :], x12_sb[:, ec, :],
                            ALU.add), "vV")
                    P.wait("scalar", "vV", va2)
                    P.wait("tensor", "vV", va2)
                    lnv2 = layer_norm_tiles(L2, v2_sb, vsq2_sb, x2_sb, psm2, psq2, psB2)
                    P.wait("sync", "vV", lnv2)
                    stx2 = P.op("sync", lambda E, ks=ks: E.dma_start(
                        out=outT.rearrange("(c p) t -> p c t", p=128)[:, :, ks],
                        in_=x2_sb[:]), "dS", 16)

                P.barrier_all()

        if debug_spills:
            P.barrier_all()
            P.op("gpsimd", lambda E: E.dma_start(out=xconv_dbg, in_=xconv_sp), "dG", 16)
            P.op("gpsimd", lambda E: E.dma_start(out=sz_dbg, in_=sz_sp), "dG", 16)
            if active("M2"):
                P.op("gpsimd", lambda E: E.dma_start(out=y2_dbg, in_=y2_sp), "dG", 16)

        fin = P.snap()
        P.wait_snap("sync", fin)

        P.flush()

    return nc



def _prep_shared(inputs):
    f32 = np.float32
    W_in = np.asarray(inputs["L0_attn_W_in"], f32)
    W_x = np.asarray(inputs["L0_attn_W_x"], f32)
    W_dt = np.asarray(inputs["L0_attn_W_dt"], f32)
    W_out = np.asarray(inputs["L0_attn_W_out"], f32)
    conv_w = np.asarray(inputs["L0_attn_conv_w"], f32).reshape(DI, 4)
    Wxp = np.zeros((128, DI), f32)
    Wxp[0:16] = W_x[RK : RK + NST]
    Wxp[32:48] = W_x[RK + NST : RK + 2 * NST]
    Wxp[64:128] = W_x[0:RK]
    sel = np.zeros((16, 16 * 128), dtype=ml_dtypes.bfloat16)
    for n in range(16):
        sel[n, n * 128 : (n + 1) * 128] = 1
    shared = {
        "W_inT": np.ascontiguousarray(W_in.T),
        "W_xTp": np.ascontiguousarray(Wxp.T),
        "W_dtT": np.ascontiguousarray(W_dt.T),
        "W_outT": np.ascontiguousarray(W_out.T),
        "w1n": np.ascontiguousarray(np.asarray(inputs["L0_ffn_w1"], f32)),
        "w2n": np.ascontiguousarray(np.asarray(inputs["L0_ffn_w2"], f32)),
        "conv_w": np.ascontiguousarray(conv_w),
        "conv_b": np.asarray(inputs["L0_attn_conv_b"], f32).reshape(DI, 1),
        "b_dt": np.asarray(inputs["L0_attn_b_dt"], f32).reshape(DI, 1),
        "A_log": np.ascontiguousarray(np.asarray(inputs["L0_attn_A_log"], f32)),
        "D_vec": np.asarray(inputs["L0_attn_D"], f32).reshape(DI, 1),
        "b1": np.asarray(inputs["L0_ffn_b1"], f32).reshape(FF, 1),
        "b2": np.asarray(inputs["L0_ffn_b2"], f32).reshape(D, 1),
        "eye": np.eye(128, dtype=ml_dtypes.bfloat16),
        "sel": sel,
    }
    return shared


def kernel(**inputs):
    x = np.asarray(inputs["L0_attn_x"], np.float32)
    shared = _prep_shared(inputs)
    nc = build_program()
    in_maps = []
    for b in range(8):
        m = dict(shared)
        m["xT"] = np.ascontiguousarray(x[b].T)
        in_maps.append(m)
    res = run_bass_kernel_spmd(nc, in_maps, core_ids=list(range(8)))
    out = np.stack([np.ascontiguousarray(res.results[b]["outT"].T) for b in range(8)])
    return out.astype(np.float32)
